# version 28
# baseline (speedup 1.0000x reference)
import numpy as np
from contextlib import ExitStack

import concourse.bass as bass
import concourse.mybir as mybir
from concourse.bass_utils import run_bass_kernel_spmd
from concourse.replica_groups import maybe_share_collective_output_space

F32 = mybir.dt.float32
BF16 = mybir.dt.bfloat16
I32 = mybir.dt.int32
ALU = mybir.AluOpType
ACT = mybir.ActivationFunctionType
AXX = mybir.AxisListType.X

C = 8
D = 2048
H = 16
SCALE = float(1.0 / np.sqrt(128.0))
NKC = 16
NEG = -3.0e38
MBIG = -1.0e30

_LN1E4_64 = np.log(10000.0) / 64.0
_S_HI = float(np.float32(_LN1E4_64))
_S_LO = float(np.float32(_LN1E4_64 - np.float64(np.float32(_LN1E4_64))))
_TWO_PI = 2.0 * np.pi
_C1 = 6.28125
_r1 = np.float64(_TWO_PI) - _C1
_C2 = float(np.float32(_r1 - (_r1 % 2.0 ** -24)))
_C3 = float(np.float32(_TWO_PI - _C1 - _C2))
_INV_2PI = float(np.float32(1.0 / _TWO_PI))
_PI = float(np.float32(np.pi))
_HALF_PI = float(np.float32(np.pi / 2.0))


class PB:

    ENG = {"t": "tensor", "v": "vector", "s": "scalar", "g": "gpsimd", "y": "sync"}

    def __init__(self, nc, es):
        self.nc = nc
        self.es = es
        self.q = {e: [] for e in self.ENG}
        self.sems = {}
        self.cnt = {}
        self.dma_chans = set()
        self.async_sems = set()
        self.changroups = {}
        self.chan_eng = {}
        self.fences = []
        for e in self.ENG:
            self._mksem("c_" + e)

    def _mksem(self, name):
        self.sems[name] = self.es.enter_context(self.nc.semaphore(name))
        self.cnt[name] = 0
        return name

    def chan(self, name, eng="g", slots=1):
        names = []
        for i in range(slots):
            n = name if slots == 1 else f"{name}_{i}"
            self._mksem(n)
            self.dma_chans.add(n)
            names.append(n)
        self.changroups[name] = names
        self.chan_eng[name] = eng
        self._rr = getattr(self, "_rr", {})
        self._rr[name] = 0
        self._last_tok = getattr(self, "_last_tok", {})
        for n in names:
            self._last_tok[n] = None
        return name

    def chan_last(self, name):
        return [self._last_tok[n] for n in self.changroups[name]
                if self._last_tok[n] is not None]

    def fence(self, toks):
        toks = list(toks)
        for g in self.changroups:
            toks += self.chan_last(g)
        self.fences = [t for t in toks if t is not None]

    def do(self, e, fn, deps=(), sem=None, k=1):
        sem = sem or ("c_" + e)
        waits = []
        drain_idx = -1
        for d in list(deps) + self.fences:
            if d is None:
                continue
            de, ds, dv, di = d
            if de == e and ds not in self.dma_chans and ds not in self.async_sems:
                drain_idx = max(drain_idx, di)
            else:
                waits.append((ds, dv))
        if sem in self.dma_chans and self.cnt[sem] > 0:
            waits.append((sem, self.cnt[sem]))
        self.cnt[sem] += k
        idx = len(self.q[e])
        self.q[e].append((waits, drain_idx, fn, sem, k))
        return (e, sem, self.cnt[sem], idx)

    def dma(self, group, out, in_, deps=()):
        names = self.changroups[group]
        slot = names[self._rr[group] % len(names)]
        self._rr[group] += 1
        tok = self.do(self.chan_eng[group], lambda eng: eng.dma_start(out=out, in_=in_),
                      deps, sem=slot, k=16)
        self._last_tok[slot] = tok
        return tok

    def dmat(self, out, in_, deps=(), sem=None):
        tok = self.do("y", lambda eng: eng.dma_start_transpose(out=out, in_=in_),
                      deps, sem="yT", k=16)
        self._last_tok["yT"] = tok
        return tok

    def emit(self):
        with self.nc.Block() as block:
            for e, name in self.ENG.items():
                ops = self.q[e]
                if not ops:
                    continue

                def body(eng, ops=ops, e=e):
                    satisfied = {}
                    drained = -1
                    for i, (waits, drain_idx, fn, sem, k) in enumerate(ops):
                        for ds, dv in waits:
                            if satisfied.get(ds, -1) < dv:
                                eng.wait_ge(self.sems[ds], dv)
                                satisfied[ds] = dv
                        if drain_idx > drained:
                            eng.drain()
                            drained = i - 1
                        instr = fn(eng)
                        if instr is not None:
                            instr.then_inc(self.sems[sem], k)
                        else:
                            eng.sem_inc(self.sems[sem], k)

                getattr(block, name)(body)


def build_program(T=8, dbg=False):
    nc = bass.Bass("TRN2", target_bir_lowering=False, debug=False)
    R = 128 * T
    NT = C * T
    SEQ = 128 * NT
    NHALF_R = max(1, R // 512)
    KRMAXH = 1024 * ((T + 1) // 2)

    xin = nc.dram_tensor("xin", [T, 128, D], F32, kind="ExternalInput").ap()
    wqT = nc.dram_tensor("wqT", [NKC, 128, D], F32, kind="ExternalInput").ap()
    woT = nc.dram_tensor("woT", [NKC, 128, D], F32, kind="ExternalInput").ap()
    wkvT = nc.dram_tensor("wkvT", [NKC, 128, 384], F32, kind="ExternalInput").ap()
    wfT = nc.dram_tensor("wfT", [NKC, 128, D], F32, kind="ExternalInput").ap()
    pos_row = nc.dram_tensor("pos_row", [1, R], F32, kind="ExternalInput").ap()
    pos_col = nc.dram_tensor("pos_col", [128, T], F32, kind="ExternalInput").ap()
    identb_in = nc.dram_tensor("identb_in", [128, 128], BF16, kind="ExternalInput").ap()
    out = nc.dram_tensor("out", [T, 128, D], F32, kind="ExternalOutput").ap()

    x1_hbm = nc.dram_tensor("x1_hbm", [T, 128, D], F32)
    x2_hbm = nc.dram_tensor("x2_hbm", [T, 128, D], F32)
    wfbf_hbm = nc.dram_tensor("wfbf_hbm", [4, NKC, 128, 512], BF16)
    khrep_hbm = nc.dram_tensor("khrep_hbm", [H, 128, SEQ], BF16)
    qT_hbm = nc.dram_tensor("qT_hbm", [H, 128, R], BF16)
    o_hbm = nc.dram_tensor("o_hbm", [T, 128, D], BF16)
    khT_hbm = nc.dram_tensor("khT_hbm", [H, SEQ], BF16)
    ex_in_bf = nc.dram_tensor("ex_in_bf", [352, R], BF16)
    alpha_hbm = nc.dram_tensor("alpha_hbm", [1, 1], F32)
    rg = [list(range(C))]
    sp = maybe_share_collective_output_space("AllGather", rg, prefer_local=True)
    ex_out_bf = nc.dram_tensor("ex_out_bf", [C, 352, R], BF16, addr_space=sp)
    if dbg:
        dbg_x1 = nc.dram_tensor("dbg_x1", [T, 128, D], F32, kind="ExternalOutput").ap()
        dbg_khvh = nc.dram_tensor("dbg_khvh", [32, SEQ], F32, kind="ExternalOutput").ap()
        dbg_o = nc.dram_tensor("dbg_o", [T, 128, D], BF16, kind="ExternalOutput").ap()
        dbg_x2 = nc.dram_tensor("dbg_x2", [T, 128, D], F32, kind="ExternalOutput").ap()
        dbg_qt = nc.dram_tensor("dbg_qt", [H, 128, R], BF16, kind="ExternalOutput").ap()

    es = ExitStack()
    pb = PB(nc, es)
    SB = lambda n, s, dt=F32: es.enter_context(nc.sbuf_tensor(n, s, dt))

    y0 = pb.chan("y0", eng="g", slots=3)
    y1 = pb.chan("y1", eng="g", slots=3)
    yT = pb.chan("yT", eng="y", slots=1)
    g0 = pb.chan("g0", eng="g", slots=3)
    g1 = pb.chan("g1", eng="g", slots=3)
    g2 = pb.chan("g2", eng="g", slots=3)
    g3 = pb.chan("g3", eng="g", slots=2)
    cc = pb._mksem("cc")
    pb.async_sems.add(cc)

    identb = SB("identb", [128, 128], BF16)
    alpha_col = SB("alpha_col", [128, 1])
    ralpha_col = SB("ralpha_col", [128, 1])
    stats = SB("stats", [128, 32])
    small = SB("small", [128, 16])
    posc = SB("posc", [128, T])
    vh_tm = SB("vh_tm", [128, NT, 16], BF16)
    o_sb = SB("o_sb", [128, 2, 128], BF16)
    ones_col = SB("ones_col", [128, 1])
    eps_col = SB("eps_col", [128, 1])
    neg1_col = SB("neg1_col", [128, 1])

    SL_RSTD, SL_BETA, SL_RBETA, SL_T0, SL_T1, SL_AL = 0, 1, 2, 3, 4, 6

    def slot(i):
        return small[:, i:i + 1]

    tok_identb = pb.dma(y0, identb[:, :], identb_in)
    tok_posc = pb.dma(y0, posc[:, :], pos_col)
    tok_ones = pb.do("v", lambda v: v.memset(ones_col[:, :], 1.0))
    tok_eps = pb.do("v", lambda v: v.memset(eps_col[:, :], 1e-5))
    tok_neg1 = pb.do("v", lambda v: v.memset(neg1_col[:, :], -1.0))


    def emit_ln(x_ap, out_ap, deps, prev=()):
        def bn(v):
            last = None
            for q in range(4):
                last = v.bn_stats(out=stats[:, 6 * q:6 * q + 6],
                                  in_=x_ap[:, 512 * q:512 * (q + 1)])
            return last
        t_bn = pb.do("v", bn, list(deps) + list(prev))
        t_ag = pb.do("v", lambda v: v.bn_aggr(out=stats[:, 24:26], in_=stats[:, 0:24]),
                     [t_bn])
        t_ln = pb.do("s", lambda s: s.activation(out=slot(SL_T0), in_=stats[:, 25:26],
                                                 func=ACT.Ln, bias=eps_col[:, :],
                                                 scale=1.0), [t_ag, tok_eps])
        t_rs = pb.do("s", lambda s: s.activation(out=slot(SL_RSTD), in_=slot(SL_T0),
                                                 func=ACT.Exp, bias=0.0, scale=-0.5), [t_ln])
        t_o = pb.do("v", lambda v: v.tensor_scalar(
            out=out_ap, in0=x_ap, scalar1=stats[:, 24:25], scalar2=slot(SL_RSTD),
            op0=ALU.subtract, op1=ALU.mult), [t_rs, t_ag] + list(deps))
        return t_o

    def emit_quant(xn_ap, xq_ap, scr_v, scr_b1, deps, prev=()):
        t_b0 = pb.do("v", lambda v: v.tensor_reduce(
            out=slot(SL_T1), in_=xn_ap, axis=AXX, op=ALU.max,
            apply_absolute_value=True), list(deps))
        t_b = pb.do("v", lambda v: v.tensor_scalar(
            out=slot(SL_BETA), in0=slot(SL_T1), scalar1=1.0 / 127.0, scalar2=1e-8,
            op0=ALU.mult, op1=ALU.max), [t_b0])
        t_rb = pb.do("v", lambda v: v.reciprocal(out=slot(SL_RBETA), in_=slot(SL_BETA)),
                     [t_b])
        b1i = scr_b1.bitcast(I32)
        H2 = [slice(0, 1024), slice(1024, 2048)]
        t_v = [None, None]; t_n0 = [None, None]; t_nm = [None, None]
        t_d1 = [None, None]; t_t1 = [None, None]; t_r = [None, None]
        t_last = [None, None]
        for hh in range(2):
            t_v[hh] = pb.do("v", lambda v, hh=hh: v.tensor_scalar(
                out=scr_v[:, H2[hh]], in0=xn_ap[:, H2[hh]], scalar1=slot(SL_RBETA),
                scalar2=-1.0, op0=ALU.mult, op1=ALU.add),
                [t_rb] + list(deps) + list(prev))
        for hh in range(2):
            t_n0[hh] = pb.do("v", lambda v, hh=hh: v.tensor_copy(
                out=b1i[:, H2[hh]], in_=scr_v[:, H2[hh]]), [t_v[hh]])
        for hh in range(2):
            t_nm[hh] = pb.do("s", lambda s, hh=hh: s.activation(
                out=xn_ap[:, H2[hh]], in_=b1i[:, H2[hh]], func=ACT.Copy,
                bias=0.0, scale=1.0), [t_n0[hh], t_b0])
        for hh in range(2):
            t_d1[hh] = pb.do("v", lambda v, hh=hh: v.scalar_tensor_tensor(
                out=scr_b1[:, H2[hh]], in0=xn_ap[:, H2[hh]], scalar=-1.0,
                in1=scr_v[:, H2[hh]], op0=ALU.mult, op1=ALU.add),
                [t_nm[hh], t_v[hh], t_n0[hh]])
        for hh in range(2):
            t_t1[hh] = pb.do("v", lambda v, hh=hh: v.scalar_tensor_tensor(
                out=xn_ap[:, H2[hh]], in0=scr_b1[:, H2[hh]], scalar=0.5,
                in1=xn_ap[:, H2[hh]], op0=ALU.is_gt, op1=ALU.add), [t_d1[hh]])
        for hh in range(2):
            t_r[hh] = pb.do("v", lambda v, hh=hh: v.scalar_tensor_tensor(
                out=scr_v[:, H2[hh]], in0=scr_b1[:, H2[hh]], scalar=-0.5,
                in1=xn_ap[:, H2[hh]], op0=ALU.is_ge, op1=ALU.add), [t_t1[hh]])
        for hh in range(2):
            t_last[hh] = pb.do("v", lambda v, hh=hh: v.tensor_scalar(
                out=xq_ap[:, H2[hh]], in0=scr_v[:, H2[hh]], scalar1=127.0,
                scalar2=-127.0, op0=ALU.min, op1=ALU.max), [t_r[hh]])
        t_qq = pb.do("v", lambda v: v.engine_nop() if False else None,
                     t_last) if False else t_last[1]
        return t_last[1], t_b

    def wbuild3(dst_bf, src_slab, deps):
        t_m1 = pb.do("v", lambda v: v.tensor_scalar(
            out=wscr[:, :], in0=src_slab, scalar1=ralpha_col[:, :], scalar2=0.5,
            op0=ALU.mult, op1=ALU.is_gt), list(deps) + [t_ralpha])
        t_m2 = pb.do("v", lambda v: v.tensor_scalar(
            out=src_slab, in0=src_slab, scalar1=ralpha_col[:, :], scalar2=-0.5,
            op0=ALU.mult, op1=ALU.is_lt), [t_m1])
        return pb.do("v", lambda v: v.tensor_tensor(
            out=dst_bf, in0=wscr[:, :], in1=src_slab, op=ALU.subtract), [t_m2])

    stA1 = ExitStack()
    SA1 = lambda n, s, dt=F32: stA1.enter_context(nc.sbuf_tensor(n, s, dt))
    slab = SA1("slab", [128, 3, D])
    stLN = ExitStack()
    ln2T = stLN.enter_context(nc.sbuf_tensor("ln2T", [128, NKC, T, 128], BF16))
    stA = ExitStack()
    SA = lambda n, s, dt=F32: stA.enter_context(nc.sbuf_tensor(n, s, dt))
    x_sb = SA("x_sb", [128, T, D])
    q_v = SA("q_v", [128, 2048]); q_b1 = SA("q_b1", [128, 2048])
    wscr = q_v
    absl = SA("absl", [128, NKC])
    xn_t = SA("xn_t", [128, 2, D])
    xq_t = SA("xq_t", [128, 2, D], BF16)
    wch = xq_t[:, 0, :]
    xqT = SA("xqT", [128, T, NKC, 128], BF16)
    fac_t = SA("fac_t", [128, T])
    ln2_t = SA("ln2_t", [128, D], BF16)
    wstr = SA("wstr", [128, 4, 1024], BF16)

    stPM = ExitStack()
    pM = stPM.enter_context(nc.psum_tensor("pM", [1, 2], F32))

    tok_slab = [None, None, None]
    tok_abs = []
    for k in range(NKC):
        tok_slab[k % 3] = pb.dma([g0, g1, g2][k % 3], slab[:, k % 3, :],
                                 wfT[k, :, :],
                                 deps=[tok_abs[k - 3]] if k >= 3 else [])
        tok_abs.append(pb.do("v", lambda v, k=k: v.tensor_reduce(
            out=absl[:, k:k + 1], in_=slab[:, k % 3, :], axis=AXX,
            op=ALU.add, apply_absolute_value=True), [tok_slab[k % 3]]))
    tok_x = [pb.dma([y0, y1][t % 2], x_sb[:, t, :], xin[t, :, :])
             for t in range(T)]
    t_abssum = pb.do("v", lambda v: v.tensor_reduce(
        out=slot(SL_T0), in_=absl[:, :], axis=AXX, op=ALU.add), tok_abs)
    t_amm = pb.do("t", lambda t: t.matmul(out=pM[0:1, 0:1], lhsT=slot(SL_T0),
                                          rhs=ones_col[:, :], start=True, stop=True),
                  [t_abssum, tok_ones])
    t_a0 = pb.do("v", lambda v: v.tensor_scalar(
        out=small[0:1, SL_AL:SL_AL + 1], in0=pM[0:1, 0:1], scalar1=1.0 / (D * D),
        scalar2=1e-8, op0=ALU.mult, op1=ALU.max), [t_amm])
    t_ad = pb.dma(g2, alpha_hbm.ap(), small[0:1, SL_AL:SL_AL + 1], [t_a0])
    t_alpha = pb.dma(g2, alpha_col[:, :], alpha_hbm.ap()[0, :].partition_broadcast(128),
                     [t_ad])
    t_ralpha = pb.do("v", lambda v: v.reciprocal(out=ralpha_col[:, :], in_=alpha_col[:, :]),
                     [t_alpha])

    tok_wf = []
    last_wbuild = [None]
    wblds = []
    for k in range(NKC):
        t_sl = pb.dma([g0, g1, g2][k % 3], slab[:, k % 3, :], wfT[k, :, :],
                      deps=([wblds[k - 3]] if k >= 3 else [tok_abs[-1]]))
        t_w = wbuild3(wch, slab[:, k % 3, :],
                      [t_sl, tok_wf[-1] if tok_wf else None] +
                      (tok_wf[-3:] if tok_wf else []))
        wblds.append(t_w)
        last_wbuild[0] = t_w
        for og in range(4):
            tok_wf.append(pb.dma(g2, wfbf_hbm[og, k, :, :],
                                 wch[:, 512 * og:512 * (og + 1)], [t_w]))

    tok_xqT = [None] * T
    tok_fac = [None] * T
    ln_toks = [None] * T
    xn_free = {0: None, 1: None}
    xq_free = {0: None, 1: None}

    def a3_quant(t):
        tb = t % 2
        t_q, t_b = emit_quant(xn_t[:, tb, :], xq_t[:, tb, :], q_v[:, :], q_b1[:, :],
                              [ln_toks[t]],
                              [xq_free[tb], last_wbuild[0]] + tok_wf[-3:])
        t_f = pb.do("v", lambda v, t=t: v.tensor_tensor(
            out=fac_t[:, t:t + 1], in0=slot(SL_BETA), in1=alpha_col[:, :],
            op=ALU.mult), [t_b, t_alpha])
        tok_fac[t] = pb.do("v", lambda v, t=t: v.tensor_scalar(
            out=fac_t[:, t:t + 1], in0=fac_t[:, t:t + 1], scalar1=0.5, scalar2=0.0,
            op0=ALU.mult, op1=ALU.add), [t_f])
        tok_xqT[t] = pb.dmat(xqT[:, t, :, :], xq_t[:, tb, :], [t_q])
        xn_free[tb] = t_q
        xq_free[tb] = tok_xqT[t]

    for t in range(T):
        tb = t % 2
        ln_toks[t] = emit_ln(x_sb[:, t, :], xn_t[:, tb, :], [tok_x[t]],
                             [xn_free[tb], ln_toks[t - 1] if t else None])
        if t >= 1:
            a3_quant(t - 1)
    a3_quant(T - 1)

    stPM.close()
    stPY = ExitStack()
    pY = stPY.enter_context(nc.psum_tensor("pY", [128, T, 512], F32))
    tok_epi = {}
    tok_wstr = [None] * 4
    for og in range(4):
        for kp in range(NKC // 2):
            kb = (og * (NKC // 2) + kp) % 4
            t_ws = pb.dma([g0, g1, g2][kb % 3],
                          wstr[:, kb, :].rearrange("p (k f) -> p k f", f=512),
                          wfbf_hbm.ap()[og, 2 * kp:2 * kp + 2, :, :].transpose([1, 0, 2]),
                          deps=[tok_wf[4 * (2 * kp) + og], tok_wf[4 * (2 * kp + 1) + og]] +
                          ([tok_wstr[kb]] if tok_wstr[kb] else []))
            tok_wstr[kb] = t_ws

            def ffngrp(te, og=og, kp=kp, kb=kb):
                last = None
                for ki in range(2):
                    k = 2 * kp + ki
                    for t in range(T):
                        last = te.matmul(out=pY[:, t, :], lhsT=xqT[:, t, k, :],
                                         rhs=wstr[:, kb, 512 * ki:512 * (ki + 1)],
                                         start=(k == 0), stop=(k == NKC - 1))
                return last
            mmdeps = [t_ws, tok_xqT[-1]]
            if kp == 0:
                mmdeps += [tok_epi.get((og - 1, t)) for t in range(T)]
            tk = pb.do("t", ffngrp, mmdeps)
            tok_wstr[kb] = tk
        for t in range(T):
            tok_epi[(og, t)] = pb.do("v", lambda v, t=t, og=og: v.scalar_tensor_tensor(
                out=x_sb[:, t, 512 * og:512 * (og + 1)], in0=pY[:, t, :],
                scalar=fac_t[:, t:t + 1], in1=x_sb[:, t, 512 * og:512 * (og + 1)],
                op0=ALU.mult, op1=ALU.add), [tk, tok_fac[t]])

    tok_x1 = [None] * T
    tok_ln2T = [None] * T
    prev_ln2 = []
    for t in range(T):
        epi4 = [tok_epi[(og, t)] for og in range(4)]
        tok_x1[t] = pb.dma(y1, x1_hbm[t, :, :], x_sb[:, t, :], epi4)
        if dbg:
            tok_x1[t] = pb.dma(g2, dbg_x1[t, :, :], x_sb[:, t, :],
                               epi4 + [tok_x1[t]])
        t_ln2 = emit_ln(x_sb[:, t, :], ln2_t[:, :], epi4, prev_ln2)
        tok_ln2T[t] = pb.dmat(ln2T[:, :, t, :], ln2_t[:, :], [t_ln2])
        prev_ln2 = [tok_ln2T[t]]

    stPY.close()
    dbg_x1_toks = []
    pb.fence([tok_ln2T[-1], tok_x1[-1]] + list(tok_epi.values()))
    stA.close()

    stB = ExitStack()
    SBB = lambda n, s, dt=F32: stB.enter_context(nc.sbuf_tensor(n, s, dt))
    inv_col = SBB("inv_col", [64, 1])
    posrep = SBB("posrep", [64, R])
    ang = SBB("ang", [64, R])
    angw = SBB("angw", [64, R])
    kkf = SBB("kkf", [64, R])
    cosT = SBB("cosT", [64, R]); sinT = SBB("sinT", [64, R])
    ropeA = SBB("ropeA", [64, R]); ropeB = SBB("ropeB", [64, R])
    qTbf = SBB("qTbf", [128, R], BF16)
    wqb = SBB("wqb", [128, NKC, D], BF16)
    wkvb = SBB("wkvb", [128, NKC, 384], BF16)
    kprT_own = SBB("kprT_own", [128, R], BF16)
    vp_own = SBB("vp_own", [128, T, 128], BF16)
    exf_own = SBB("exf_own", [32, R])
    exb2 = SBB("exb2", [32, 2, R], BF16)
    exbkv = SBB("exbkv", [32, R], BF16)
    stQ = ExitStack()
    pQ = stQ.enter_context(nc.psum_tensor("pQ", [128, 2, 1024], F32))
    kkfi = kkf.bitcast(I32)

    t_j = pb.do("g", lambda g: g.iota(out=inv_col[:, :], pattern=[[0, 1]], base=0,
                                      channel_multiplier=1,
                                      allow_small_or_imprecise_dtypes=True))
    t_i1 = pb.do("v", lambda v: v.tensor_scalar(out=kkf[0:64, 0:1], in0=inv_col[:, :],
                                                scalar1=-_S_HI, scalar2=0.0,
                                                op0=ALU.mult, op1=ALU.add), [t_j])
    t_i2 = pb.do("v", lambda v: v.scalar_tensor_tensor(
        out=inv_col[:, :], in0=inv_col[:, :], scalar=-_S_LO, in1=kkf[0:64, 0:1],
        op0=ALU.mult, op1=ALU.add), [t_i1, t_j])
    t_inv = pb.do("s", lambda s: s.activation(out=inv_col[:, :], in_=inv_col[:, :],
                                              func=ACT.Exp, bias=0.0, scale=1.0), [t_i2])
    t_pr = pb.dma(g0, posrep[:, :], pos_row[0, :].partition_broadcast(64))
    t_ang = pb.do("v", lambda v: v.tensor_scalar(
        out=ang[:, :], in0=posrep[:, :], scalar1=inv_col[:, :], scalar2=0.0,
        op0=ALU.mult, op1=ALU.add), [t_pr, t_inv])

    def emit_trig(dst, shift, deps):
        t1 = pb.do("v", lambda v: v.tensor_scalar(
            out=kkf[:, :], in0=ang[:, :], scalar1=_INV_2PI, scalar2=0.0,
            op0=ALU.mult, op1=ALU.add), deps)
        t2 = pb.do("v", lambda v: v.tensor_copy(out=kkfi[:, :], in_=kkf[:, :]), [t1])
        t3 = pb.do("v", lambda v: v.tensor_copy(out=kkf[:, :], in_=kkfi[:, :]), [t2])
        t4 = pb.do("v", lambda v: v.scalar_tensor_tensor(
            out=angw[:, :], in0=kkf[:, :], scalar=-_C1, in1=ang[:, :],
            op0=ALU.mult, op1=ALU.add), [t3] + deps)
        t5 = pb.do("v", lambda v: v.scalar_tensor_tensor(
            out=angw[:, :], in0=kkf[:, :], scalar=-_C2, in1=angw[:, :],
            op0=ALU.mult, op1=ALU.add), [t4])
        t6 = pb.do("v", lambda v: v.scalar_tensor_tensor(
            out=angw[:, :], in0=kkf[:, :], scalar=-_C3, in1=angw[:, :],
            op0=ALU.mult, op1=ALU.add), [t5])
        if shift is not None:
            t6 = pb.do("v", lambda v: v.tensor_scalar(
                out=angw[:, :], in0=angw[:, :], scalar1=shift, scalar2=0.0,
                op0=ALU.add, op1=ALU.add), [t6])
        t7 = pb.do("v", lambda v: v.tensor_scalar(
            out=kkf[:, :], in0=angw[:, :], scalar1=_PI, scalar2=-_TWO_PI,
            op0=ALU.is_gt, op1=ALU.mult), [t6])
        t8 = pb.do("v", lambda v: v.tensor_tensor(
            out=angw[:, :], in0=angw[:, :], in1=kkf[:, :], op=ALU.add), [t7, t6])
        return pb.do("s", lambda s: s.activation(out=dst[:, :], in_=angw[:, :],
                                                 func=ACT.Sin, bias=0.0, scale=1.0), [t8])

    t_sin = emit_trig(sinT, None, [t_ang])
    t_cos = emit_trig(cosT, _HALF_PI, [t_ang, t_sin])

    def emit_rope(pqs, dstT, deps):
        t_r1 = pb.do("v", lambda v: v.tensor_tensor(
            out=ropeA[:, :], in0=pqs(0, 64), in1=cosT[:, :], op=ALU.mult),
            deps + [t_cos])
        t_r2 = pb.do("v", lambda v: v.tensor_tensor(
            out=ropeB[:, :], in0=pqs(64, 128), in1=sinT[:, :], op=ALU.mult),
            deps + [t_sin])
        t_r3 = pb.do("v", lambda v: v.tensor_tensor(
            out=dstT[0:64, 0:R], in0=ropeA[:, :], in1=ropeB[:, :], op=ALU.subtract),
            [t_r1, t_r2])
        t_r4 = pb.do("v", lambda v: v.tensor_tensor(
            out=ropeA[:, :], in0=pqs(64, 128), in1=cosT[:, :], op=ALU.mult),
            [t_r2, t_r3] + deps)
        t_r5 = pb.do("v", lambda v: v.tensor_tensor(
            out=ropeB[:, :], in0=pqs(0, 64), in1=sinT[:, :], op=ALU.mult),
            [t_r3] + deps)
        t_r6 = pb.do("v", lambda v: v.tensor_tensor(
            out=dstT[64:128, 0:R], in0=ropeA[:, :], in1=ropeB[:, :], op=ALU.add),
            [t_r4, t_r5])
        return [t_r3, t_r6]

    def proj_group(lhsT_fn, bank, deps0):
        def grp(te):
            last = None
            for k in range(NKC):
                for hh in range(NHALF_R):
                    w = min(512, R - 512 * hh)
                    last = te.matmul(
                        out=pQ[:, bank, 512 * hh:512 * hh + w],
                        lhsT=lhsT_fn(k),
                        rhs=ln2T[:, k, :, :].rearrange("p t f -> p (t f)")[:, 512 * hh:512 * hh + w],
                        start=(k == 0), stop=(k == NKC - 1))
            return last
        return pb.do("t", grp, deps0)

    tok_wkv = []
    for k in range(NKC):
        t_sl = pb.dma(g0, slab[:, k % 2, 0:384], wkvT[k, :, :],
                      deps=([tok_wkv[k - 2]] if k >= 2 else tok_wf[-3:]))
        tok_wkv.append(pb.do("v", lambda v, k=k: v.tensor_copy(
            out=wkvb[:, k, :], in_=slab[:, k % 2, 0:384]), [t_sl]))

    tk0 = proj_group(lambda k: wkvb[:, k, 0:128], 0, [tok_wkv[-1], tok_ln2T[-1]])
    tok_kp = emit_rope(lambda a, b: pQ[a:b, 0, 0:R], kprT_own, [tk0])
    tk1 = proj_group(lambda k: wkvb[:, k, 128:256], 1, [tok_wkv[-1], tok_ln2T[-1]])
    t_vc = pb.do("v", lambda v: v.tensor_copy(out=qTbf[:, 0:R], in_=pQ[:, 1, 0:R]),
                 [tk1])
    t_vtr = pb.dmat(vp_own[:, :, :], qTbf[:, 0:R], [t_vc])
    tk2 = proj_group(lambda k: wkvb[:, k, 256:384], 0, [tok_kp[1]])
    t_kvc = pb.do("v", lambda v: v.tensor_copy(out=exbkv[:, 0:R],
                                               in_=pQ[0:32, 0, 0:R]), [tk2])
    t_sg = pb.do("s", lambda s: s.activation(out=exf_own[0:32, 0:R],
                                             in_=pQ[32:64, 0, 0:R], func=ACT.Sigmoid,
                                             bias=0.0, scale=1.0), [tk2])
    t_ls = pb.do("s", lambda s: s.activation(out=exf_own[0:32, 0:R],
                                             in_=exf_own[0:32, 0:R], func=ACT.Ln,
                                             bias=0.0, scale=1.0), [t_sg])
    t_hi = pb.do("v", lambda v: v.tensor_copy(out=exb2[:, 0, 0:R],
                                              in_=exf_own[0:32, 0:R]), [t_ls])
    t_lo = pb.do("v", lambda v: v.scalar_tensor_tensor(
        out=exb2[:, 1, 0:R], in0=exb2[:, 0, 0:R], scalar=-1.0,
        in1=exf_own[0:32, 0:R], op0=ALU.mult, op1=ALU.add), [t_hi])

    t_e1 = pb.dma(g0, ex_in_bf.ap()[0:128, :], kprT_own[:, 0:R], tok_kp)
    t_e2 = pb.dma(g0, ex_in_bf.ap()[128:256, :],
                  vp_own[:, :, :].rearrange("p t f -> p (t f)"), [t_vtr])
    t_e3 = pb.dma(g0, ex_in_bf.ap()[256:320, :]
                  .rearrange("(p c) f -> p c f", c=2), exb2[:, :, 0:R],
                  [t_hi, t_lo])
    t_e4 = pb.dma(g0, ex_in_bf.ap()[320:352, :], exbkv[:, 0:R], [t_kvc])
    t_cc2 = pb.do("g", lambda g: g.collective_compute(
        "AllGather", ALU.bypass, replica_groups=rg, ins=[ex_in_bf.ap().opt()],
        outs=[ex_out_bf.ap().opt()]), [t_e1, t_e2, t_e3, t_e4], sem=cc)
    t_cc1 = t_cc2

    tok_wqb = []
    for k in range(NKC):
        t_sl = pb.dma(g1, slab[:, k % 2, :], wqT[k, :, :],
                      deps=([tok_wqb[k - 2]] if k >= 2 else [tok_wkv[-1]]))
        tok_wqb.append(pb.do("v", lambda v, k=k: v.tensor_copy(
            out=wqb[:, k, :], in_=slab[:, k % 2, :]), [t_sl]))
    tok_qT = [None] * H
    pq_free = [[t_kvc, t_sg], [t_vc]]
    qtbf_free = t_vtr
    for hd in range(H):
        bank = hd % 2
        tk = proj_group(lambda k, hd=hd: wqb[:, k, 128 * hd:128 * (hd + 1)], bank,
                        [tok_wqb[-1]] + (pq_free[bank] if isinstance(pq_free[bank], list)
                                         else [pq_free[bank]]))
        r36 = emit_rope(lambda a, b, bank=bank: pQ[a:b, bank, 0:R], qTbf,
                        [tk] + ([qtbf_free] if qtbf_free else []))
        pq_free[bank] = r36[1]
        tok_qT[hd] = pb.dma(y1, qT_hbm[hd, :, 0:R], qTbf[:, 0:R], r36)
        qtbf_free = tok_qT[hd]
        if dbg:
            qtbf_free = pb.dma(g2, dbg_qt[hd, :, :], qTbf[:, 0:R],
                               r36 + [tok_qT[hd]])

    fenceB0 = [t_cc2, t_cc1, tok_kp[1], t_vtr, t_ls, tok_qT[-1]]
    for pf in pq_free:
        fenceB0 += pf if isinstance(pf, list) else [pf]
    pb.fence(fenceB0)
    stQ.close()
    stB.close()
    stLN.close()
    stA1.close()

    stBC = ExitStack()
    SBC = lambda n, s, dt=F32: stBC.enter_context(nc.sbuf_tensor(n, s, dt))
    masks = SBC("masks", [128, 1024], BF16)
    kprT_all = SBC("kprT_all", [128, NT, 128], BF16)
    vp_all = SBC("vp_all", [128, NT, 128], BF16)

    stIO = ExitStack()
    iota_sb = stIO.enter_context(nc.sbuf_tensor("iota_sb", [128, 1024], F32))
    tok_iota = pb.do("g", lambda g: g.iota(out=iota_sb[:, :], pattern=[[1, 1024]], base=0,
                                           channel_multiplier=0,
                                           allow_small_or_imprecise_dtypes=True))
    t_mask = pb.do("v", lambda v: v.tensor_scalar(
        out=masks[:, :], in0=iota_sb[:, :], scalar1=posc[:, 0:1], scalar2=MBIG,
        op0=ALU.is_gt, op1=ALU.mult), [tok_posc, tok_iota])
    tok_masks = [t_mask] * T

    stB2 = ExitStack()
    SB2 = lambda n, s, dt=F32: stB2.enter_context(nc.sbuf_tensor(n, s, dt))
    glog = SB2("glog", [32, SEQ])
    gkv = SB2("gkv", [32, SEQ])
    E2 = SB2("E2", [32, SEQ])
    khbf = SB2("khbf", [32, SEQ], BF16)

    tok_gather = []
    tok_kvgather = []
    tok_gkv = []
    E2bb = E2[:, :].bitcast(BF16)
    for ci in range(C):
        tok_kvgather.append(pb.dma(
            y0, kprT_all[:, ci::C, :],
            ex_out_bf.ap()[ci, 0:128, :].rearrange("p (t f) -> p t f", f=128),
            [t_cc1]))
        tok_kvgather.append(pb.dma(
            y1, vp_all[:, ci::C, :],
            ex_out_bf.ap()[ci, 128:256, :].rearrange("p (t f) -> p t f", f=128),
            [t_cc1]))
    for ci in range(C):
        tok_gather.append(pb.dma(
            g0, E2bb[:, 0:SEQ].rearrange("p (t f) -> p t f", f=128)[:, ci::C, :],
            ex_out_bf.ap()[ci, 256:320, :].rearrange("(p c) f -> p c f", c=2)[:, 0, :]
            .rearrange("p (t f) -> p t f", f=128), [t_cc2]))
        tok_gather.append(pb.dma(
            g2, E2bb[:, SEQ:2 * SEQ].rearrange("p (t f) -> p t f", f=128)[:, ci::C, :],
            ex_out_bf.ap()[ci, 256:320, :].rearrange("(p c) f -> p c f", c=2)[:, 1, :]
            .rearrange("p (t f) -> p t f", f=128), [t_cc2]))
        tok_gkv.append(pb.dma(
            g3, khbf[:, :].rearrange("p (t f) -> p t f", f=128)[:, ci::C, :],
            ex_out_bf.ap()[ci, 320:352, :].rearrange("p (t f) -> p t f", f=128),
            [t_cc2]))

    t_comb = pb.do("v", lambda v: v.tensor_tensor(
        out=glog[:, :], in0=E2bb[:, 0:SEQ], in1=E2bb[:, SEQ:2 * SEQ],
        op=ALU.add), tok_gather)
    t_gkvf = pb.do("s", lambda s: s.activation(
        out=gkv[:, :], in_=khbf[:, :], func=ACT.Copy, bias=0.0, scale=1.0), tok_gkv)
    t_f1 = pb.do("v", lambda v: v.tensor_tensor_scan(
        out=glog[:, :], data0=glog[:, :], data1=glog[:, :], initial=0.0,
        op0=ALU.add, op1=ALU.bypass), [t_comb])
    t_f2 = pb.do("s", lambda s: s.activation(
        out=E2[:, 1:SEQ], in_=glog[:, 0:SEQ - 1], func=ACT.Exp, bias=0.0, scale=1.0),
        [t_f1])
    t_f3 = pb.do("v", lambda v: v.memset(E2[:, 0:1], 1.0), [t_f1])
    t_f4 = pb.do("v", lambda v: v.tensor_tensor(
        out=E2[:, :], in0=gkv[:, :], in1=E2[:, :], op=ALU.mult),
        [t_f3, t_f2, t_gkvf])
    t_f5 = pb.do("v", lambda v: v.tensor_tensor_scan(
        out=glog[:, :], data0=E2[:, :], data1=E2[:, :], initial=0.0,
        op0=ALU.add, op1=ALU.bypass), [t_f4])
    khvh = glog
    t_dbgkh = None
    if dbg:
        t_dbgkh = pb.dma(g2, dbg_khvh[:, :], khvh[:, :], [t_f5])

    t_khb = pb.do("v", lambda v: v.tensor_copy(out=khbf[:, :], in_=khvh[:, :]), [t_f5])
    t_khd = pb.dma(g0, khT_hbm.ap()[:, :], khbf[16:32, :], [t_khb])
    t_vht = pb.dmat(vh_tm[:, :, :], khbf[0:16, :], [t_khb])
    tok_khbc = {}
    HSPL = 1024 * max(1, T // 2)
    khbc0a = [None]

    def emit_khbc(h_):
        if h_ < H and h_ not in tok_khbc:
            if h_ == 0 and HSPL < SEQ:
                khbc0a[0] = pb.dma(
                    g3, khrep_hbm[0, :, 0:HSPL],
                    khT_hbm.ap()[0, 0:HSPL].partition_broadcast(128), [t_khd])
                tok_khbc[0] = pb.dma(
                    g3, khrep_hbm[0, :, HSPL:SEQ],
                    khT_hbm.ap()[0, HSPL:SEQ].partition_broadcast(128), [t_khd])
            else:
                tok_khbc[h_] = pb.dma(
                    g3, khrep_hbm[h_, :, :],
                    khT_hbm.ap()[h_, :].partition_broadcast(128), [t_khd])
    emit_khbc(0)
    emit_khbc(1)

    pb.fence([t_khd, t_vht, t_khb, tok_masks[-1], t_f5, t_dbgkh, t_comb, t_gkvf]
             + tok_gather + tok_gkv)
    stB2.close()
    stIO.close()

    stC = ExitStack()
    SC = lambda n, s, dt=F32: stC.enter_context(nc.sbuf_tensor(n, s, dt))
    PC = lambda n, s, dt=F32: stC.enter_context(nc.psum_tensor(n, s, dt))
    panel = SC("panel", [128, 2, KRMAXH])
    ptil = SC("ptil", [128, 2, 2, KRMAXH], BF16)
    PT = SC("PT", [128, 2, 2, KRMAXH // 128, 128], BF16)
    qTb = SC("qTb", [128, 2, R], BF16)
    KhrT = SC("KhrT", [128, 2, SEQ], BF16)
    Vh = SC("Vh", [128, 2, NT, 128], BF16)
    mslot = SC("mslot", [128, 2 * T])
    msc = SC("msc", [128, 2, 12])
    osc = SC("osc", [128, 128])
    pS = PC("pS", [128, 2, 1024])
    pO = PC("pO", [128, 2, 2, 512])
    AM_A, AM_B, AM_NA, AM_NB, AM_LA, AM_LB, AM_RA, AM_RB, AM_M, AM_NM, AM_LI, AM_T2 = range(12)

    def msl(i):
        return msc[:, i:i + 1]

    def emit_builds_dma(h, dk, dkr):
        par = h % 2
        emit_khbc(h + 1)
        return pb.dma(g1, KhrT[:, par, :], khrep_hbm[h, :, :],
                      dk + dkr + [tok_khbc[h]])

    def emit_builds_compute(h, t_kh, dvh):
        par = h % 2
        t_k = pb.do("v", lambda v, par=par: v.scalar_tensor_tensor(
            out=KhrT[:, par, :], in0=kprT_all[:, :, :].rearrange("p t f -> p (t f)"),
            scalar=SCALE, in1=KhrT[:, par, :], op0=ALU.mult, op1=ALU.mult),
            [t_kh] + tok_kvgather)
        t_v = pb.do("v", lambda v, h=h, par=par: v.tensor_tensor(
            out=Vh[:, par, :, :], in0=vp_all[:, :, :],
            in1=vh_tm[:, :, h:h + 1].broadcast_to([128, NT, 128]), op=ALU.mult),
            [t_vht] + dvh + tok_kvgather)
        return t_kh, t_k, t_v

    if HSPL < SEQ:
        NTH = HSPL // 128
        emit_khbc(1)
        _b0a = pb.dma(g1, KhrT[:, 0, 0:HSPL], khrep_hbm[0, :, 0:HSPL],
                      [khbc0a[0]])
        _b0 = pb.dma(g1, KhrT[:, 0, HSPL:SEQ], khrep_hbm[0, :, HSPL:SEQ],
                     [tok_khbc[0]])
        t_k_a = pb.do("v", lambda v: v.scalar_tensor_tensor(
            out=KhrT[:, 0, 0:HSPL],
            in0=kprT_all[:, 0:NTH, :].rearrange("p t f -> p (t f)"),
            scalar=SCALE, in1=KhrT[:, 0, 0:HSPL], op0=ALU.mult, op1=ALU.mult),
            [_b0a] + tok_kvgather)
        t_k_b = pb.do("v", lambda v: v.scalar_tensor_tensor(
            out=KhrT[:, 0, HSPL:SEQ],
            in0=kprT_all[:, NTH:NT, :].rearrange("p t f -> p (t f)"),
            scalar=SCALE, in1=KhrT[:, 0, HSPL:SEQ], op0=ALU.mult, op1=ALU.mult),
            [_b0] + tok_kvgather)
        t_v_a = pb.do("v", lambda v: v.tensor_tensor(
            out=Vh[:, 0, 0:NTH, :], in0=vp_all[:, 0:NTH, :],
            in1=vh_tm[:, 0:NTH, 0:1].broadcast_to([128, NTH, 128]), op=ALU.mult),
            [t_vht] + tok_kvgather)
        t_v_b = pb.do("v", lambda v: v.tensor_tensor(
            out=Vh[:, 0, NTH:NT, :], in0=vp_all[:, NTH:NT, :],
            in1=vh_tm[:, NTH:NT, 0:1].broadcast_to([128, NT - NTH, 128]),
            op=ALU.mult), [t_vht] + tok_kvgather)
        builds = {0: (_b0, {"a": t_k_a, "b": t_k_b}, {"a": t_v_a, "b": t_v_b})}
    else:
        _b0 = emit_builds_dma(0, [t_khd], [t_cc1])
        builds = {0: emit_builds_compute(0, _b0, [t_cc1])}

    def kdep(h, ck):
        b = builds[h][1]
        if isinstance(b, dict):
            return b["a"] if 1024 * (ck + 1) <= HSPL else b["b"]
        return b

    def vdep(h, c0, nch):
        b = builds[h][2]
        if isinstance(b, dict):
            return b["a"] if 1024 * (c0 + nch) <= HSPL else b["b"]
        return b
    tok_q_load = {0: pb.dma(y0, qTb[:, 0, 0:R], qT_hbm[0, :, 0:R], [tok_qT[0]])}

    last_qk_of_head = {}
    tok_exp = {0: None, 1: None}
    tok_pvdone = {}
    tok_odma = {}
    tok_ptil_free = {(a, b): None for a in range(2) for b in range(2)}
    tok_PT_free = {(a, b): None for a in range(2) for b in range(2)}
    vh_last_use = {0: None, 1: None}
    evac_by_bank = {0: None, 1: None}
    o_dma_toks = []
    pending_tail = [None]
    pending_tail2 = [None]
    osc_last = [None]

    def msl2(ip, i):
        return msc[:, ip, i:i + 1]

    def emit_front(h, j, mid_hook=None):
        par = h % 2
        ip = (h * T + j) % 2
        gp = ip
        n = j + 1
        ha = (n + 1) // 2
        halves = [(0, ha)] if n == ha else [(0, ha), (ha, n)]
        half_info = []
        last_mm = None
        for hi, (c0, c1) in enumerate(halves):
            nch = c1 - c0
            ev_toks = []
            for ci, ck in enumerate(range(c0, c1)):
                bank = ck % 2

                def qkgrp(te, j=j, ck=ck, bank=bank, par=par):
                    last = None
                    for s2 in range(2):
                        last = te.matmul(
                            out=pS[:, bank, 512 * s2:512 * (s2 + 1)],
                            lhsT=qTb[:, par, 128 * j:128 * (j + 1)],
                            rhs=KhrT[:, par, 1024 * ck + 512 * s2:1024 * ck + 512 * (s2 + 1)],
                            start=True, stop=(ck != j))
                        if ck == j:
                            last = te.matmul(
                                out=pS[:, bank, 512 * s2:512 * (s2 + 1)],
                                lhsT=identb[:, :],
                                rhs=masks[:, 512 * s2:512 * (s2 + 1)],
                                start=False, stop=True)
                    return last
                mmdeps = [kdep(h, ck), tok_q_load[h], evac_by_bank[bank]]
                if ck == j:
                    mmdeps += [tok_identb, tok_masks[j]]
                last_mm = pb.do("t", qkgrp, mmdeps)
                t_ev = pb.do("v", lambda v, hi=hi, ci=ci, bank=bank: v.tensor_scalar(
                    out=panel[:, hi, 1024 * ci:1024 * (ci + 1)], in0=pS[:, bank, :],
                    scalar1=1.0, scalar2=NEG, op0=ALU.mult, op1=ALU.max,
                    accum_out=mslot[:, hi * T + ci:hi * T + ci + 1]),
                    [last_mm, tok_exp[hi]])
                ev_toks.append(t_ev)
                evac_by_bank[bank] = t_ev
            nfin = AM_NA if hi == 0 else AM_NB
            lfin = AM_LA if hi == 0 else AM_LB
            t_nm = pb.do("v", lambda v, hi=hi, nch=nch, nfin=nfin, ip=ip:
                         v.tensor_reduce(
                out=msl2(ip, nfin), in_=mslot[:, hi * T:hi * T + nch], axis=AXX,
                op=ALU.max, negate=True), ev_toks)
            kr = 1024 * nch
            t_exp = pb.do("s", lambda s, hi=hi, kr=kr, nfin=nfin, lfin=lfin, ip=ip:
                          s.activation(
                out=ptil[:, ip, hi, 0:kr], in_=panel[:, hi, 0:kr], func=ACT.Exp,
                bias=msl2(ip, nfin), scale=1.0, accum_out=msl2(ip, lfin)),
                [t_nm, tok_ptil_free[(ip, hi)]] + ev_toks)
            tok_exp[hi] = t_exp
            t_tr = pb.dmat(PT[:, ip, hi, 0:kr // 128, :], ptil[:, ip, hi, 0:kr],
                           [t_exp, tok_PT_free[(ip, hi)]])
            tok_ptil_free[(ip, hi)] = t_tr
            half_info.append((t_tr, t_exp, nch, hi, c0))
            if hi == 0 and mid_hook is not None:
                mid_hook()
        last_qk_of_head[h] = last_mm
        return (h, j, par, ip, gp, halves, half_info)

    def emit_tail_stats(state):
        (h, j, par, ip, gp, halves, half_info) = state
        if len(halves) == 1:
            return None
        t_nm2 = pb.do("v", lambda v, ip=ip: v.tensor_tensor(
            out=msl2(ip, AM_NM), in0=msl2(ip, AM_NA), in1=msl2(ip, AM_NB),
            op=ALU.min), [half_info[0][1], half_info[1][1]])
        t_ra = pb.do("s", lambda s, ip=ip: s.activation(
            out=msl2(ip, AM_RA), in_=msl2(ip, AM_NA), func=ACT.Exp,
            bias=msl2(ip, AM_NM), scale=-1.0), [t_nm2, half_info[0][1]])
        t_rb = pb.do("s", lambda s, ip=ip: s.activation(
            out=msl2(ip, AM_RB), in_=msl2(ip, AM_NB), func=ACT.Exp,
            bias=msl2(ip, AM_NM), scale=-1.0), [t_ra, t_nm2, half_info[1][1]])
        t_l1 = pb.do("s", lambda s, ip=ip: s.activation(
            out=msl2(ip, AM_T2), in_=msl2(ip, AM_LA), func=ACT.Copy, bias=0.0,
            scale=msl2(ip, AM_RA)), [t_ra, half_info[0][1]])
        return (t_ra, t_rb, t_l1)

    def emit_tail(state, stats):
        (h, j, par, ip, gp, halves, half_info) = state
        pv_toks = []
        for (t_tr, t_exp2, nch, hi, c0) in half_info:
            def pvgrp(te, hi=hi, nch=nch, c0=c0, gp=gp, par=par, ip=ip):
                last = None
                for kt in range(nch * 8):
                    last = te.matmul(
                        out=pO[:, gp, hi, 0:128], lhsT=PT[:, ip, hi, kt, :],
                        rhs=Vh[:, par, c0 * 8 + kt, :],
                        start=(kt == 0), stop=(kt == nch * 8 - 1))
                return last
            t_pv = pb.do("t", pvgrp, [t_tr, vdep(h, c0, nch), tok_pvdone.get(gp)])
            pv_toks.append(t_pv)
            tok_PT_free[(ip, hi)] = t_pv
            vh_last_use[par] = t_pv
        lslot = AM_LI if ((h * T + j) >> 1) & 1 == 0 else AM_M
        if len(halves) == 1:
            t_li = pb.do("v", lambda v, ip=ip, lslot=lslot: v.reciprocal(
                out=msl2(ip, lslot), in_=msl2(ip, AM_LA)), [half_info[0][1]])
            stats2 = (None, None, t_li, lslot)
        else:
            (t_ra, t_rb, t_l1) = stats
            t_l2 = pb.do("v", lambda v, ip=ip, lslot=lslot: v.scalar_tensor_tensor(
                out=msl2(ip, lslot), in0=msl2(ip, AM_LB), scalar=msl2(ip, AM_RB),
                in1=msl2(ip, AM_T2), op0=ALU.mult, op1=ALU.add),
                [t_rb, t_l1, half_info[1][1]])
            t_li = pb.do("v", lambda v, ip=ip, lslot=lslot: v.reciprocal(
                out=msl2(ip, lslot), in_=msl2(ip, lslot)), [t_l2])
            stats2 = (t_ra, t_rb, t_li, lslot)
        return (h, j, par, ip, gp, halves, half_info, stats2, pv_toks)

    def emit_tail2(state2):
        (h, j, par, ip, gp, halves, half_info, stats, pv_toks) = state2
        (t_ra, t_rb, t_li, lslot) = stats
        if len(halves) == 1:
            t_oe = pb.do("v", lambda v, gp=gp, ip=ip, lslot=lslot: v.tensor_scalar(
                out=o_sb[:, gp, :], in0=pO[:, gp, 0, 0:128], scalar1=msl2(ip, lslot),
                scalar2=0.0, op0=ALU.mult, op1=ALU.add),
                [t_li, tok_odma.get(gp)] + pv_toks)
        else:
            t_o1 = pb.do("v", lambda v, gp=gp, ip=ip: v.tensor_scalar(
                out=osc[:, :], in0=pO[:, gp, 0, 0:128], scalar1=msl2(ip, AM_RA),
                scalar2=0.0, op0=ALU.mult, op1=ALU.add),
                [t_ra, osc_last[0]] + pv_toks[:1])
            t_o2 = pb.do("v", lambda v, gp=gp, ip=ip: v.scalar_tensor_tensor(
                out=osc[:, :], in0=pO[:, gp, 1, 0:128], scalar=msl2(ip, AM_RB),
                in1=osc[:, :], op0=ALU.mult, op1=ALU.add),
                [t_rb, t_o1] + pv_toks[1:])
            t_oe = pb.do("v", lambda v, gp=gp, ip=ip, lslot=lslot: v.tensor_scalar(
                out=o_sb[:, gp, :], in0=osc[:, :], scalar1=msl2(ip, lslot),
                scalar2=0.0, op0=ALU.mult, op1=ALU.add),
                [t_li, t_o2, tok_odma.get(gp)])
            osc_last[0] = t_oe
        tok_pvdone[gp] = t_oe
        tok_odma[gp] = pb.dma(y1, o_hbm[j, :, 128 * h:128 * (h + 1)],
                              o_sb[:, gp, :], [t_oe])
        o_dma_toks.append(tok_odma[gp])

    for h in range(H):
        if h + 1 < H:
            tok_q_load[h + 1] = pb.dma(y0, qTb[:, (h + 1) % 2, 0:R],
                                       qT_hbm[h + 1, :, 0:R],
                                       [tok_qT[h + 1], last_qk_of_head.get(h - 1)])
        for j in range(T):
            box = {}

            def hook(box=box):
                if pending_tail[0] is not None:
                    box["stats"] = emit_tail_stats(pending_tail[0][0])
            st = emit_front(h, j, mid_hook=hook)
            st2 = None
            if pending_tail[0] is not None:
                st2 = emit_tail(pending_tail[0][0], box.get("stats"))
            if pending_tail2[0] is not None:
                emit_tail2(pending_tail2[0])
            pending_tail2[0] = st2
            pending_tail[0] = (st,)
            if j == min(1, T - 1) and h + 1 < H:
                builds[h + 1] = (emit_builds_dma(
                    h + 1, [last_qk_of_head.get(h - 1)],
                    [last_qk_of_head.get(h - 1)]),)
            if j == min(5, T - 1) and h + 1 < H:
                builds[h + 1] = emit_builds_compute(
                    h + 1, builds[h + 1][0], [vh_last_use.get((h + 1) % 2)])
    _fs = emit_tail_stats(pending_tail[0][0])
    _st2 = emit_tail(pending_tail[0][0], _fs)
    if pending_tail2[0] is not None:
        emit_tail2(pending_tail2[0])
    emit_tail2(_st2)
    if dbg:
        for t in range(T):
            pb.dma(g2, dbg_o[t, :, :], o_hbm[t, :, :], o_dma_toks)

    pb.fence(o_dma_toks + [tok_pvdone.get(0), tok_pvdone.get(1)])

    stC.close()
    stBC.close()

    stD = ExitStack()
    SD = lambda n, s, dt=F32: stD.enter_context(nc.sbuf_tensor(n, s, dt))
    PD = lambda n, s, dt=F32: stD.enter_context(nc.psum_tensor(n, s, dt))
    wob = SD("wob", [128, NKC, D], BF16)
    oT = SD("oT", [128, 2, NKC, 128], BF16)
    x1t = SD("x1t", [128, 2, D])
    x2t = SD("x2t", [128, 2, D])
    xn3 = SD("xn3", [128, 2, D])
    d_v = SD("d_v", [128, 2048]); d_b1 = SD("d_b1", [128, 2048])
    xq3 = SD("xq3", [128, 2, D], BF16)
    xq3T = SD("xq3T", [128, 2, NKC, 128], BF16)
    fac2 = SD("fac2", [128, T])
    slabD = SD("slabD", [128, 3, D])
    pD = PD("pD", [128, 2, 512])

    tok_wob = []
    for k in range(NKC):
        t_sl = pb.dma([g0, g1, g2][k % 3], slabD[:, k % 3, :], woT[k, :, :],
                      deps=[tok_wob[k - 3]] if k >= 3 else [])
        tok_wob.append(pb.do("v", lambda v, k=k: v.tensor_copy(
            out=wob[:, k, :], in_=slabD[:, k % 3, :]), [t_sl]))

    tok_x2 = [None] * T
    d2_mm_last = [None] * 4
    SPT = NKC * 128 // R
    bankD_free = [None, None]
    d2_free = {0: None, 1: None}
    d3_ln_toks = [None] * T
    d3_xn_free = {0: None, 1: None}
    d3_xq_free = {0: None, 1: None}
    xq3T_free = {0: None, 1: None}
    x2_readers = {0: None, 1: None}
    tok_xq_sp = [None] * T
    tok_fac2 = [None] * T

    def d2_frontf(t):
        tb = t % 2
        t_ot = pb.dmat(oT[:, tb, :, :], o_hbm[t, :, :],
                       o_dma_toks[-3:] + [d2_free[tb]])
        t_x1l = pb.dma(y0, x1t[:, tb, :], x1_hbm[t, :, :], [tok_x1[t], d2_free[tb]])
        return (t, tb, t_ot, t_x1l)

    def d2_tailf(state):
        t, tb, t_ot, t_x1l = state
        epi = []
        for og in range(4):
            bank = og % 2

            def mmo(te, og=og, bank=bank, tb=tb):
                last = None
                for k in range(NKC):
                    last = te.matmul(out=pD[:, bank, :], lhsT=oT[:, tb, k, :],
                                     rhs=wob[:, k, 512 * og:512 * (og + 1)],
                                     start=(k == 0), stop=(k == NKC - 1))
                return last
            tk = pb.do("t", mmo, [t_ot, bankD_free[bank]] + tok_wob[-3:])
            d2_mm_last[og] = tk
            t_epi = pb.do("v", lambda v, og=og, bank=bank, tb=tb: v.scalar_tensor_tensor(
                out=x2t[:, tb, 512 * og:512 * (og + 1)], in0=pD[:, bank, :], scalar=1.0,
                in1=x1t[:, tb, 512 * og:512 * (og + 1)], op0=ALU.mult, op1=ALU.add),
                [tk, t_x1l, tok_x2[t - 2] if t >= 2 else None, x2_readers[tb]])
            epi.append(t_epi)
            bankD_free[bank] = t_epi
        tok_x2[t] = pb.dma(y1, x2_hbm[t, :, :], x2t[:, tb, :], epi)
        if dbg:
            tok_x2[t] = pb.dma(g2, dbg_x2[t, :, :], x2t[:, tb, :], epi + [tok_x2[t]])
        d2_free[tb] = epi[-1]
        return epi

    def d3_prep(t, epi):
        tb = t % 2
        t_lno = emit_ln(x2t[:, tb, :], xn3[:, tb, :], [epi[-1]],
                        [d3_xn_free[tb], d3_ln_toks[t - 1] if t else None])
        d3_ln_toks[t] = t_lno
        x2_readers[tb] = t_lno
        t_q, t_b = emit_quant(xn3[:, tb, :], xq3[:, tb, :], d_v[:, :], d_b1[:, :],
                              [t_lno], [d3_xq_free[tb]])
        d3_xn_free[tb] = t_q
        t_f = pb.do("v", lambda v, t=t: v.tensor_tensor(
            out=fac2[:, t:t + 1], in0=slot(SL_BETA), in1=alpha_col[:, :], op=ALU.mult),
            [t_b])
        t_f2 = pb.do("v", lambda v, t=t: v.tensor_scalar(
            out=fac2[:, t:t + 1], in0=fac2[:, t:t + 1], scalar1=0.5, scalar2=0.0,
            op0=ALU.mult, op1=ALU.add), [t_f])
        tok_fac2[t] = t_f2
        t_tr = pb.dmat(xq3T[:, tb, :, :], xq3[:, tb, :], [t_q, xq3T_free[tb]])
        d3_xq_free[tb] = t_tr
        tok_xq_sp[t] = pb.dma(
            g2, qT_hbm.ap()[SPT * t:SPT * (t + 1), :, :]
            .rearrange("a p (c f) -> p a c f", f=128),
            xq3T[:, tb, :, :].rearrange("p (a c) f -> p a c f", a=SPT), [t_tr])
        xq3T_free[tb] = tok_xq_sp[t]

    pend = None
    for t in range(T):
        st = d2_frontf(t)
        if pend is not None:
            ep = d2_tailf(pend)
            d3_prep(pend[0], ep)
        pend = st
    ep = d2_tailf(pend)
    d3_prep(pend[0], ep)
    t_d2last = bankD_free[1] if bankD_free[1] is not None else bankD_free[0]

    tok_out = [None] * T
    d3_free = {0: None, 1: None}
    d3_x1_free = dict(d2_free)
    d3_xqT_free = dict(xq3T_free)
    x2w_guard = {0: [tok_x2[T - 2] if T >= 2 else None,
                     d3_ln_toks[T - 2] if T >= 2 else None],
                 1: [tok_x2[T - 1], d3_ln_toks[T - 1]]}

    def d3_frontf(t):
        tb = t % 2
        t_xr = pb.dma(
            g3, xq3T[:, tb, :, :].rearrange("p (a c) f -> p a c f", a=SPT),
            qT_hbm.ap()[SPT * t:SPT * (t + 1), :, :]
            .rearrange("a p (c f) -> p a c f", f=128),
            [tok_xq_sp[t], d3_xqT_free[tb]])
        t_x2r = pb.dma(y0, x1t[:, tb, :], x2_hbm[t, :, :],
                       [tok_x2[t], d3_x1_free[tb]])
        return (t, tb, t_xr, t_x2r)

    def d3_tailf(state):
        t, tb, t_xr, t_x2r = state
        epi = []
        for og in range(4):
            bank = og % 2

            def mmf(te, og=og, bank=bank, tb=tb):
                last = None
                for k in range(NKC):
                    last = te.matmul(out=pD[:, bank, :], lhsT=xq3T[:, tb, k, :],
                                     rhs=wob[:, k, 512 * og:512 * (og + 1)],
                                     start=(k == 0), stop=(k == NKC - 1))
                return last
            tk = pb.do("t", mmf, [t_xr, bankD_free[bank]] + wqf2_og[og])
            t_epi = pb.do("v", lambda v, og=og, bank=bank, t=t, tb=tb:
                          v.scalar_tensor_tensor(
                out=x2t[:, tb, 512 * og:512 * (og + 1)], in0=pD[:, bank, :],
                scalar=fac2[:, t:t + 1], in1=x1t[:, tb, 512 * og:512 * (og + 1)],
                op0=ALU.mult, op1=ALU.add),
                [tk, t_x2r, tok_fac2[t], d3_free[tb]] + x2w_guard[tb])
            epi.append(t_epi)
            bankD_free[bank] = t_epi
        d3_xqT_free[tb] = epi[-1]
        tok_out[t] = pb.dma(y1, out[t, :, :], x2t[:, tb, :], epi)
        d3_free[tb] = tok_out[t]
        d3_x1_free[tb] = epi[-1]
        return tok_out[t]

    pend = d3_frontf(0)

    tok_wqf2 = []
    wqf2_og = [None] * 4
    for og in range(4):
        for k in range(NKC):
            tok_wqf2.append(pb.dma(
                g1, wob[:, k, 512 * og:512 * (og + 1)], wfbf_hbm[og, k, :, :],
                deps=(tok_wob[-3:] + [d2_mm_last[og]]) if k == 0 else
                [d2_mm_last[og]] if k <= 2 else []))
        wqf2_og[og] = tok_wqf2[-3:]

    for t in range(1, T):
        st = d3_frontf(t)
        d3_tailf(pend)
        pend = st
    d3_tailf(pend)

    pb.do("y", lambda y: None, tok_out)

    pb.emit()
    stD.close()
    es.close()
    return nc


def shard_inputs(L0_attn_x, L0_attn_Wq, L0_attn_Wkv, L0_attn_Wkvh, L0_attn_Wg,
                 L0_attn_Wout, L0_ffn_weight, T=8):
    x = np.asarray(L0_attn_x, np.float32).reshape(C * T, 128, D)
    wq = np.ascontiguousarray(np.asarray(L0_attn_Wq, np.float32).T).reshape(NKC, 128, D)
    wo = np.ascontiguousarray(np.asarray(L0_attn_Wout, np.float32).T).reshape(NKC, 128, D)
    wf = np.ascontiguousarray(np.asarray(L0_ffn_weight, np.float32).T).reshape(NKC, 128, D)
    wkvh = np.asarray(L0_attn_Wkvh, np.float32)
    wg = np.asarray(L0_attn_Wg, np.float32)
    perm = np.concatenate([np.arange(16, 32), np.arange(0, 16)])
    wkv_cat = np.concatenate([np.asarray(L0_attn_Wkv, np.float32),
                              wkvh[perm], wg[perm],
                              np.zeros((64, D), np.float32)], axis=0)
    wkvt = np.ascontiguousarray(wkv_cat.T).reshape(NKC, 128, 384)
    import ml_dtypes
    identb = np.eye(128, dtype=ml_dtypes.bfloat16)
    in_maps = []
    for c in range(C):
        tiles = x[c::C]
        gidx = (np.arange(T) * C + c)
        pos = (gidx[:, None] * 128 + np.arange(128)[None, :]).astype(np.float32)
        in_maps.append({
            "xin": np.ascontiguousarray(tiles),
            "wqT": wq, "woT": wo, "wkvT": wkvt, "wfT": wf,
            "pos_row": np.ascontiguousarray(pos.reshape(1, T * 128)),
            "pos_col": np.ascontiguousarray(pos.T),
            "identb_in": identb,
        })
    return in_maps


def kernel(L0_attn_x, L0_attn_Wq, L0_attn_Wkv, L0_attn_Wkvh, L0_attn_Wg,
           L0_attn_Wout, L0_ffn_weight):
    T = 8
    in_maps = shard_inputs(L0_attn_x, L0_attn_Wq, L0_attn_Wkv, L0_attn_Wkvh,
                           L0_attn_Wg, L0_attn_Wout, L0_ffn_weight, T=T)
    nc = build_program(T=T)
    res = run_bass_kernel_spmd(nc, in_maps, core_ids=list(range(C)))
    full = np.empty((C * T, 128, D), np.float32)
    for c in range(C):
        full[c::C] = np.asarray(res.results[c]["out"], np.float32)
    return full.reshape(1, C * T * 128, D)


# version 29
# speedup vs baseline: 1.0097x; 1.0097x over previous
import numpy as np
from contextlib import ExitStack

import concourse.bass as bass
import concourse.mybir as mybir
from concourse.bass_utils import run_bass_kernel_spmd
from concourse.replica_groups import maybe_share_collective_output_space

F32 = mybir.dt.float32
BF16 = mybir.dt.bfloat16
I32 = mybir.dt.int32
ALU = mybir.AluOpType
ACT = mybir.ActivationFunctionType
AXX = mybir.AxisListType.X

C = 8
D = 2048
H = 16
SCALE = float(1.0 / np.sqrt(128.0))
NKC = 16
NEG = -3.0e38
MBIG = -1.0e30

_LN1E4_64 = np.log(10000.0) / 64.0
_S_HI = float(np.float32(_LN1E4_64))
_S_LO = float(np.float32(_LN1E4_64 - np.float64(np.float32(_LN1E4_64))))
_TWO_PI = 2.0 * np.pi
_C1 = 6.28125
_r1 = np.float64(_TWO_PI) - _C1
_C2 = float(np.float32(_r1 - (_r1 % 2.0 ** -24)))
_C3 = float(np.float32(_TWO_PI - _C1 - _C2))
_INV_2PI = float(np.float32(1.0 / _TWO_PI))
_PI = float(np.float32(np.pi))
_HALF_PI = float(np.float32(np.pi / 2.0))


class PB:

    ENG = {"t": "tensor", "v": "vector", "s": "scalar", "g": "gpsimd", "y": "sync"}

    def __init__(self, nc, es):
        self.nc = nc
        self.es = es
        self.q = {e: [] for e in self.ENG}
        self.sems = {}
        self.cnt = {}
        self.dma_chans = set()
        self.async_sems = set()
        self.changroups = {}
        self.chan_eng = {}
        self.fences = []
        for e in self.ENG:
            self._mksem("c_" + e)

    def _mksem(self, name):
        self.sems[name] = self.es.enter_context(self.nc.semaphore(name))
        self.cnt[name] = 0
        return name

    def chan(self, name, eng="g", slots=1):
        names = []
        for i in range(slots):
            n = name if slots == 1 else f"{name}_{i}"
            self._mksem(n)
            self.dma_chans.add(n)
            names.append(n)
        self.changroups[name] = names
        self.chan_eng[name] = eng
        self._rr = getattr(self, "_rr", {})
        self._rr[name] = 0
        self._last_tok = getattr(self, "_last_tok", {})
        for n in names:
            self._last_tok[n] = None
        return name

    def chan_last(self, name):
        return [self._last_tok[n] for n in self.changroups[name]
                if self._last_tok[n] is not None]

    def fence(self, toks):
        toks = list(toks)
        for g in self.changroups:
            toks += self.chan_last(g)
        self.fences = [t for t in toks if t is not None]

    def do(self, e, fn, deps=(), sem=None, k=1):
        sem = sem or ("c_" + e)
        waits = []
        drain_idx = -1
        for d in list(deps) + self.fences:
            if d is None:
                continue
            de, ds, dv, di = d
            if de == e and ds not in self.dma_chans and ds not in self.async_sems:
                drain_idx = max(drain_idx, di)
            else:
                waits.append((ds, dv))
        if sem in self.dma_chans and self.cnt[sem] > 0:
            waits.append((sem, self.cnt[sem]))
        self.cnt[sem] += k
        idx = len(self.q[e])
        self.q[e].append((waits, drain_idx, fn, sem, k))
        return (e, sem, self.cnt[sem], idx)

    def dma(self, group, out, in_, deps=()):
        names = self.changroups[group]
        slot = names[self._rr[group] % len(names)]
        self._rr[group] += 1
        tok = self.do(self.chan_eng[group], lambda eng: eng.dma_start(out=out, in_=in_),
                      deps, sem=slot, k=16)
        self._last_tok[slot] = tok
        return tok

    def dmat(self, out, in_, deps=(), sem=None):
        tok = self.do("y", lambda eng: eng.dma_start_transpose(out=out, in_=in_),
                      deps, sem="yT", k=16)
        self._last_tok["yT"] = tok
        return tok

    def emit(self):
        with self.nc.Block() as block:
            for e, name in self.ENG.items():
                ops = self.q[e]
                if not ops:
                    continue

                def body(eng, ops=ops, e=e):
                    satisfied = {}
                    drained = -1
                    for i, (waits, drain_idx, fn, sem, k) in enumerate(ops):
                        for ds, dv in waits:
                            if satisfied.get(ds, -1) < dv:
                                eng.wait_ge(self.sems[ds], dv)
                                satisfied[ds] = dv
                        if drain_idx > drained:
                            eng.drain()
                            drained = i - 1
                        instr = fn(eng)
                        if instr is not None:
                            instr.then_inc(self.sems[sem], k)
                        else:
                            eng.sem_inc(self.sems[sem], k)

                getattr(block, name)(body)


def build_program(T=8, dbg=False):
    nc = bass.Bass("TRN2", target_bir_lowering=False, debug=False)
    R = 128 * T
    NT = C * T
    SEQ = 128 * NT
    NHALF_R = max(1, R // 512)
    KRMAXH = 1024 * ((T + 1) // 2)

    xin = nc.dram_tensor("xin", [T, 128, D], F32, kind="ExternalInput").ap()
    wqT = nc.dram_tensor("wqT", [NKC, 128, D], F32, kind="ExternalInput").ap()
    woT = nc.dram_tensor("woT", [NKC, 128, D], F32, kind="ExternalInput").ap()
    wkvT = nc.dram_tensor("wkvT", [NKC, 128, 384], F32, kind="ExternalInput").ap()
    wfT = nc.dram_tensor("wfT", [NKC, 128, D], F32, kind="ExternalInput").ap()
    pos_row = nc.dram_tensor("pos_row", [1, R], F32, kind="ExternalInput").ap()
    pos_col = nc.dram_tensor("pos_col", [128, T], F32, kind="ExternalInput").ap()
    identb_in = nc.dram_tensor("identb_in", [128, 128], BF16, kind="ExternalInput").ap()
    out = nc.dram_tensor("out", [T, 128, D], F32, kind="ExternalOutput").ap()

    x1_hbm = nc.dram_tensor("x1_hbm", [T, 128, D], F32)
    x2_hbm = nc.dram_tensor("x2_hbm", [T, 128, D], F32)
    wfbf_hbm = nc.dram_tensor("wfbf_hbm", [4, NKC, 128, 512], BF16)
    khrep_hbm = nc.dram_tensor("khrep_hbm", [H, 128, SEQ], BF16)
    qT_hbm = nc.dram_tensor("qT_hbm", [H, 128, R], BF16)
    o_hbm = nc.dram_tensor("o_hbm", [T, 128, D], BF16)
    khT_hbm = nc.dram_tensor("khT_hbm", [H, SEQ], BF16)
    ex_in_bf = nc.dram_tensor("ex_in_bf", [352, R], BF16)
    alpha_hbm = nc.dram_tensor("alpha_hbm", [1, 1], F32)
    rg = [list(range(C))]
    sp = maybe_share_collective_output_space("AllGather", rg, prefer_local=True)
    ex_out_bf = nc.dram_tensor("ex_out_bf", [C, 352, R], BF16, addr_space=sp)
    if dbg:
        dbg_x1 = nc.dram_tensor("dbg_x1", [T, 128, D], F32, kind="ExternalOutput").ap()
        dbg_khvh = nc.dram_tensor("dbg_khvh", [32, SEQ], F32, kind="ExternalOutput").ap()
        dbg_o = nc.dram_tensor("dbg_o", [T, 128, D], BF16, kind="ExternalOutput").ap()
        dbg_x2 = nc.dram_tensor("dbg_x2", [T, 128, D], F32, kind="ExternalOutput").ap()
        dbg_qt = nc.dram_tensor("dbg_qt", [H, 128, R], BF16, kind="ExternalOutput").ap()

    es = ExitStack()
    pb = PB(nc, es)
    SB = lambda n, s, dt=F32: es.enter_context(nc.sbuf_tensor(n, s, dt))

    y0 = pb.chan("y0", eng="g", slots=3)
    y1 = pb.chan("y1", eng="y", slots=3)
    yT = pb.chan("yT", eng="y", slots=1)
    g0 = pb.chan("g0", eng="g", slots=3)
    g1 = pb.chan("g1", eng="g", slots=3)
    g2 = pb.chan("g2", eng="g", slots=3)
    g3 = pb.chan("g3", eng="g", slots=2)
    cc = pb._mksem("cc")
    pb.async_sems.add(cc)

    identb = SB("identb", [128, 128], BF16)
    alpha_col = SB("alpha_col", [128, 1])
    ralpha_col = SB("ralpha_col", [128, 1])
    stats = SB("stats", [128, 32])
    small = SB("small", [128, 16])
    posc = SB("posc", [128, T])
    vh_tm = SB("vh_tm", [128, NT, 16], BF16)
    o_sb = SB("o_sb", [128, 2, 128], BF16)
    ones_col = SB("ones_col", [128, 1])
    eps_col = SB("eps_col", [128, 1])
    neg1_col = SB("neg1_col", [128, 1])

    SL_RSTD, SL_BETA, SL_RBETA, SL_T0, SL_T1, SL_AL = 0, 1, 2, 3, 4, 6

    def slot(i):
        return small[:, i:i + 1]

    tok_identb = pb.dma(y0, identb[:, :], identb_in)
    tok_posc = pb.dma(y0, posc[:, :], pos_col)
    tok_ones = pb.do("v", lambda v: v.memset(ones_col[:, :], 1.0))
    tok_eps = pb.do("v", lambda v: v.memset(eps_col[:, :], 1e-5))
    tok_neg1 = pb.do("v", lambda v: v.memset(neg1_col[:, :], -1.0))


    def emit_ln(x_ap, out_ap, deps, prev=()):
        def bn(v):
            last = None
            for q in range(4):
                last = v.bn_stats(out=stats[:, 6 * q:6 * q + 6],
                                  in_=x_ap[:, 512 * q:512 * (q + 1)])
            return last
        t_bn = pb.do("v", bn, list(deps) + list(prev))
        t_ag = pb.do("v", lambda v: v.bn_aggr(out=stats[:, 24:26], in_=stats[:, 0:24]),
                     [t_bn])
        t_ln = pb.do("s", lambda s: s.activation(out=slot(SL_T0), in_=stats[:, 25:26],
                                                 func=ACT.Ln, bias=eps_col[:, :],
                                                 scale=1.0), [t_ag, tok_eps])
        t_rs = pb.do("s", lambda s: s.activation(out=slot(SL_RSTD), in_=slot(SL_T0),
                                                 func=ACT.Exp, bias=0.0, scale=-0.5), [t_ln])
        t_o = pb.do("v", lambda v: v.tensor_scalar(
            out=out_ap, in0=x_ap, scalar1=stats[:, 24:25], scalar2=slot(SL_RSTD),
            op0=ALU.subtract, op1=ALU.mult), [t_rs, t_ag] + list(deps))
        return t_o

    def emit_quant(xn_ap, xq_ap, scr_v, scr_b1, deps, prev=()):
        t_b0 = pb.do("v", lambda v: v.tensor_reduce(
            out=slot(SL_T1), in_=xn_ap, axis=AXX, op=ALU.max,
            apply_absolute_value=True), list(deps))
        t_b = pb.do("v", lambda v: v.tensor_scalar(
            out=slot(SL_BETA), in0=slot(SL_T1), scalar1=1.0 / 127.0, scalar2=1e-8,
            op0=ALU.mult, op1=ALU.max), [t_b0])
        t_rb = pb.do("v", lambda v: v.reciprocal(out=slot(SL_RBETA), in_=slot(SL_BETA)),
                     [t_b])
        b1i = scr_b1.bitcast(I32)
        H2 = [slice(0, 1024), slice(1024, 2048)]
        t_v = [None, None]; t_n0 = [None, None]; t_nm = [None, None]
        t_d1 = [None, None]; t_t1 = [None, None]; t_r = [None, None]
        t_last = [None, None]
        for hh in range(2):
            t_v[hh] = pb.do("v", lambda v, hh=hh: v.tensor_scalar(
                out=scr_v[:, H2[hh]], in0=xn_ap[:, H2[hh]], scalar1=slot(SL_RBETA),
                scalar2=-1.0, op0=ALU.mult, op1=ALU.add),
                [t_rb] + list(deps) + list(prev))
        for hh in range(2):
            t_n0[hh] = pb.do("v", lambda v, hh=hh: v.tensor_copy(
                out=b1i[:, H2[hh]], in_=scr_v[:, H2[hh]]), [t_v[hh]])
        for hh in range(2):
            t_nm[hh] = pb.do("s", lambda s, hh=hh: s.activation(
                out=xn_ap[:, H2[hh]], in_=b1i[:, H2[hh]], func=ACT.Copy,
                bias=0.0, scale=1.0), [t_n0[hh], t_b0])
        for hh in range(2):
            t_d1[hh] = pb.do("v", lambda v, hh=hh: v.scalar_tensor_tensor(
                out=scr_b1[:, H2[hh]], in0=xn_ap[:, H2[hh]], scalar=-1.0,
                in1=scr_v[:, H2[hh]], op0=ALU.mult, op1=ALU.add),
                [t_nm[hh], t_v[hh], t_n0[hh]])
        for hh in range(2):
            t_t1[hh] = pb.do("v", lambda v, hh=hh: v.scalar_tensor_tensor(
                out=xn_ap[:, H2[hh]], in0=scr_b1[:, H2[hh]], scalar=0.5,
                in1=xn_ap[:, H2[hh]], op0=ALU.is_gt, op1=ALU.add), [t_d1[hh]])
        for hh in range(2):
            t_r[hh] = pb.do("v", lambda v, hh=hh: v.scalar_tensor_tensor(
                out=scr_v[:, H2[hh]], in0=scr_b1[:, H2[hh]], scalar=-0.5,
                in1=xn_ap[:, H2[hh]], op0=ALU.is_ge, op1=ALU.add), [t_t1[hh]])
        for hh in range(2):
            t_last[hh] = pb.do("v", lambda v, hh=hh: v.tensor_scalar(
                out=xq_ap[:, H2[hh]], in0=scr_v[:, H2[hh]], scalar1=127.0,
                scalar2=-127.0, op0=ALU.min, op1=ALU.max), [t_r[hh]])
        t_qq = pb.do("v", lambda v: v.engine_nop() if False else None,
                     t_last) if False else t_last[1]
        return t_last[1], t_b

    def wbuild3(dst_bf, src_slab, deps):
        t_m1 = pb.do("v", lambda v: v.tensor_scalar(
            out=wscr[:, :], in0=src_slab, scalar1=ralpha_col[:, :], scalar2=0.5,
            op0=ALU.mult, op1=ALU.is_gt), list(deps) + [t_ralpha])
        t_m2 = pb.do("v", lambda v: v.tensor_scalar(
            out=src_slab, in0=src_slab, scalar1=ralpha_col[:, :], scalar2=-0.5,
            op0=ALU.mult, op1=ALU.is_lt), [t_m1])
        return pb.do("v", lambda v: v.tensor_tensor(
            out=dst_bf, in0=wscr[:, :], in1=src_slab, op=ALU.subtract), [t_m2])

    stA1 = ExitStack()
    SA1 = lambda n, s, dt=F32: stA1.enter_context(nc.sbuf_tensor(n, s, dt))
    slab = SA1("slab", [128, 3, D])
    stLN = ExitStack()
    ln2T = stLN.enter_context(nc.sbuf_tensor("ln2T", [128, NKC, T, 128], BF16))
    stA = ExitStack()
    SA = lambda n, s, dt=F32: stA.enter_context(nc.sbuf_tensor(n, s, dt))
    x_sb = SA("x_sb", [128, T, D])
    q_v = SA("q_v", [128, 2048]); q_b1 = SA("q_b1", [128, 2048])
    wscr = q_v
    absl = SA("absl", [128, NKC])
    xn_t = SA("xn_t", [128, 2, D])
    xq_t = SA("xq_t", [128, 2, D], BF16)
    wch = xq_t[:, 0, :]
    xqT = SA("xqT", [128, T, NKC, 128], BF16)
    fac_t = SA("fac_t", [128, T])
    ln2_t = SA("ln2_t", [128, D], BF16)
    wstr = SA("wstr", [128, 4, 1024], BF16)

    stPM = ExitStack()
    pM = stPM.enter_context(nc.psum_tensor("pM", [1, 2], F32))

    tok_x = [pb.dma(y0, x_sb[:, t, :], xin[t, :, :]) for t in range(T)]

    tok_slab = [None, None, None]
    tok_abs = []
    for k in range(NKC):
        tok_slab[k % 3] = pb.dma([g0, g1, g2][k % 3], slab[:, k % 3, :],
                                 wfT[k, :, :],
                                 deps=[tok_abs[k - 3]] if k >= 3 else [])
        tok_abs.append(pb.do("v", lambda v, k=k: v.tensor_reduce(
            out=absl[:, k:k + 1], in_=slab[:, k % 3, :], axis=AXX,
            op=ALU.add, apply_absolute_value=True), [tok_slab[k % 3]]))
    t_abssum = pb.do("v", lambda v: v.tensor_reduce(
        out=slot(SL_T0), in_=absl[:, :], axis=AXX, op=ALU.add), tok_abs)
    t_amm = pb.do("t", lambda t: t.matmul(out=pM[0:1, 0:1], lhsT=slot(SL_T0),
                                          rhs=ones_col[:, :], start=True, stop=True),
                  [t_abssum, tok_ones])
    t_a0 = pb.do("v", lambda v: v.tensor_scalar(
        out=small[0:1, SL_AL:SL_AL + 1], in0=pM[0:1, 0:1], scalar1=1.0 / (D * D),
        scalar2=1e-8, op0=ALU.mult, op1=ALU.max), [t_amm])
    t_ad = pb.dma(g2, alpha_hbm.ap(), small[0:1, SL_AL:SL_AL + 1], [t_a0])
    t_alpha = pb.dma(g2, alpha_col[:, :], alpha_hbm.ap()[0, :].partition_broadcast(128),
                     [t_ad])
    t_ralpha = pb.do("v", lambda v: v.reciprocal(out=ralpha_col[:, :], in_=alpha_col[:, :]),
                     [t_alpha])

    tok_wf = []
    last_wbuild = [None]
    wblds = []
    for k in range(NKC):
        t_sl = pb.dma([g0, g1, g2][k % 3], slab[:, k % 3, :], wfT[k, :, :],
                      deps=([wblds[k - 3]] if k >= 3 else [tok_abs[-1]]))
        t_w = wbuild3(wch, slab[:, k % 3, :],
                      [t_sl, tok_wf[-1] if tok_wf else None] +
                      (tok_wf[-3:] if tok_wf else []))
        wblds.append(t_w)
        last_wbuild[0] = t_w
        for og in range(4):
            tok_wf.append(pb.dma(g2, wfbf_hbm[og, k, :, :],
                                 wch[:, 512 * og:512 * (og + 1)], [t_w]))

    tok_xqT = [None] * T
    tok_fac = [None] * T
    ln_toks = [None] * T
    xn_free = {0: None, 1: None}
    xq_free = {0: None, 1: None}

    def a3_quant(t):
        tb = t % 2
        t_q, t_b = emit_quant(xn_t[:, tb, :], xq_t[:, tb, :], q_v[:, :], q_b1[:, :],
                              [ln_toks[t]],
                              [xq_free[tb], last_wbuild[0]] + tok_wf[-3:])
        t_f = pb.do("v", lambda v, t=t: v.tensor_tensor(
            out=fac_t[:, t:t + 1], in0=slot(SL_BETA), in1=alpha_col[:, :],
            op=ALU.mult), [t_b, t_alpha])
        tok_fac[t] = pb.do("v", lambda v, t=t: v.tensor_scalar(
            out=fac_t[:, t:t + 1], in0=fac_t[:, t:t + 1], scalar1=0.5, scalar2=0.0,
            op0=ALU.mult, op1=ALU.add), [t_f])
        tok_xqT[t] = pb.dmat(xqT[:, t, :, :], xq_t[:, tb, :], [t_q])
        xn_free[tb] = t_q
        xq_free[tb] = tok_xqT[t]

    for t in range(T):
        tb = t % 2
        ln_toks[t] = emit_ln(x_sb[:, t, :], xn_t[:, tb, :], [tok_x[t]],
                             [xn_free[tb], ln_toks[t - 1] if t else None])
        if t >= 1:
            a3_quant(t - 1)
    a3_quant(T - 1)

    stPM.close()
    stPY = ExitStack()
    pY = stPY.enter_context(nc.psum_tensor("pY", [128, T, 512], F32))
    tok_epi = {}
    tok_wstr = [None] * 4
    for og in range(4):
        for kp in range(NKC // 2):
            kb = (og * (NKC // 2) + kp) % 4
            t_ws = pb.dma([g0, g1, g2][kb % 3],
                          wstr[:, kb, :].rearrange("p (k f) -> p k f", f=512),
                          wfbf_hbm.ap()[og, 2 * kp:2 * kp + 2, :, :].transpose([1, 0, 2]),
                          deps=[tok_wf[4 * (2 * kp) + og], tok_wf[4 * (2 * kp + 1) + og]] +
                          ([tok_wstr[kb]] if tok_wstr[kb] else []))
            tok_wstr[kb] = t_ws

            def ffngrp(te, og=og, kp=kp, kb=kb):
                last = None
                for ki in range(2):
                    k = 2 * kp + ki
                    for t in range(T):
                        last = te.matmul(out=pY[:, t, :], lhsT=xqT[:, t, k, :],
                                         rhs=wstr[:, kb, 512 * ki:512 * (ki + 1)],
                                         start=(k == 0), stop=(k == NKC - 1))
                return last
            mmdeps = [t_ws, tok_xqT[-1]]
            if kp == 0:
                mmdeps += [tok_epi.get((og - 1, t)) for t in range(T)]
            tk = pb.do("t", ffngrp, mmdeps)
            tok_wstr[kb] = tk
        for t in range(T):
            tok_epi[(og, t)] = pb.do("v", lambda v, t=t, og=og: v.scalar_tensor_tensor(
                out=x_sb[:, t, 512 * og:512 * (og + 1)], in0=pY[:, t, :],
                scalar=fac_t[:, t:t + 1], in1=x_sb[:, t, 512 * og:512 * (og + 1)],
                op0=ALU.mult, op1=ALU.add), [tk, tok_fac[t]])

    tok_x1 = [None] * T
    tok_ln2T = [None] * T
    prev_ln2 = []
    for t in range(T):
        epi4 = [tok_epi[(og, t)] for og in range(4)]
        tok_x1[t] = pb.dma(y1, x1_hbm[t, :, :], x_sb[:, t, :], epi4)
        if dbg:
            tok_x1[t] = pb.dma(g2, dbg_x1[t, :, :], x_sb[:, t, :],
                               epi4 + [tok_x1[t]])
        t_ln2 = emit_ln(x_sb[:, t, :], ln2_t[:, :], epi4, prev_ln2)
        tok_ln2T[t] = pb.dmat(ln2T[:, :, t, :], ln2_t[:, :], [t_ln2])
        prev_ln2 = [tok_ln2T[t]]

    stPY.close()
    dbg_x1_toks = []
    pb.fence([tok_ln2T[-1], tok_x1[-1]] + list(tok_epi.values()))
    stA.close()

    stB = ExitStack()
    SBB = lambda n, s, dt=F32: stB.enter_context(nc.sbuf_tensor(n, s, dt))
    inv_col = SBB("inv_col", [64, 1])
    posrep = SBB("posrep", [64, R])
    ang = SBB("ang", [64, R])
    angw = SBB("angw", [64, R])
    kkf = SBB("kkf", [64, R])
    cosT = SBB("cosT", [64, R]); sinT = SBB("sinT", [64, R])
    ropeA = SBB("ropeA", [64, R]); ropeB = SBB("ropeB", [64, R])
    qTbf = SBB("qTbf", [128, R], BF16)
    wqb = SBB("wqb", [128, NKC, D], BF16)
    wkvb = SBB("wkvb", [128, NKC, 384], BF16)
    kprT_own = SBB("kprT_own", [128, R], BF16)
    vp_own = SBB("vp_own", [128, T, 128], BF16)
    exf_own = SBB("exf_own", [32, R])
    exb2 = SBB("exb2", [32, 2, R], BF16)
    exbkv = SBB("exbkv", [32, R], BF16)
    stQ = ExitStack()
    pQ = stQ.enter_context(nc.psum_tensor("pQ", [128, 2, 1024], F32))
    kkfi = kkf.bitcast(I32)

    t_j = pb.do("g", lambda g: g.iota(out=inv_col[:, :], pattern=[[0, 1]], base=0,
                                      channel_multiplier=1,
                                      allow_small_or_imprecise_dtypes=True))
    t_i1 = pb.do("v", lambda v: v.tensor_scalar(out=kkf[0:64, 0:1], in0=inv_col[:, :],
                                                scalar1=-_S_HI, scalar2=0.0,
                                                op0=ALU.mult, op1=ALU.add), [t_j])
    t_i2 = pb.do("v", lambda v: v.scalar_tensor_tensor(
        out=inv_col[:, :], in0=inv_col[:, :], scalar=-_S_LO, in1=kkf[0:64, 0:1],
        op0=ALU.mult, op1=ALU.add), [t_i1, t_j])
    t_inv = pb.do("s", lambda s: s.activation(out=inv_col[:, :], in_=inv_col[:, :],
                                              func=ACT.Exp, bias=0.0, scale=1.0), [t_i2])
    t_pr = pb.dma(g0, posrep[:, :], pos_row[0, :].partition_broadcast(64))
    t_ang = pb.do("v", lambda v: v.tensor_scalar(
        out=ang[:, :], in0=posrep[:, :], scalar1=inv_col[:, :], scalar2=0.0,
        op0=ALU.mult, op1=ALU.add), [t_pr, t_inv])

    def emit_trig(dst, shift, deps):
        t1 = pb.do("v", lambda v: v.tensor_scalar(
            out=kkf[:, :], in0=ang[:, :], scalar1=_INV_2PI, scalar2=0.0,
            op0=ALU.mult, op1=ALU.add), deps)
        t2 = pb.do("v", lambda v: v.tensor_copy(out=kkfi[:, :], in_=kkf[:, :]), [t1])
        t3 = pb.do("v", lambda v: v.tensor_copy(out=kkf[:, :], in_=kkfi[:, :]), [t2])
        t4 = pb.do("v", lambda v: v.scalar_tensor_tensor(
            out=angw[:, :], in0=kkf[:, :], scalar=-_C1, in1=ang[:, :],
            op0=ALU.mult, op1=ALU.add), [t3] + deps)
        t5 = pb.do("v", lambda v: v.scalar_tensor_tensor(
            out=angw[:, :], in0=kkf[:, :], scalar=-_C2, in1=angw[:, :],
            op0=ALU.mult, op1=ALU.add), [t4])
        t6 = pb.do("v", lambda v: v.scalar_tensor_tensor(
            out=angw[:, :], in0=kkf[:, :], scalar=-_C3, in1=angw[:, :],
            op0=ALU.mult, op1=ALU.add), [t5])
        if shift is not None:
            t6 = pb.do("v", lambda v: v.tensor_scalar(
                out=angw[:, :], in0=angw[:, :], scalar1=shift, scalar2=0.0,
                op0=ALU.add, op1=ALU.add), [t6])
        t7 = pb.do("v", lambda v: v.tensor_scalar(
            out=kkf[:, :], in0=angw[:, :], scalar1=_PI, scalar2=-_TWO_PI,
            op0=ALU.is_gt, op1=ALU.mult), [t6])
        t8 = pb.do("v", lambda v: v.tensor_tensor(
            out=angw[:, :], in0=angw[:, :], in1=kkf[:, :], op=ALU.add), [t7, t6])
        return pb.do("s", lambda s: s.activation(out=dst[:, :], in_=angw[:, :],
                                                 func=ACT.Sin, bias=0.0, scale=1.0), [t8])

    t_sin = emit_trig(sinT, None, [t_ang])
    t_cos = emit_trig(cosT, _HALF_PI, [t_ang, t_sin])

    def emit_rope(pqs, dstT, deps):
        t_r1 = pb.do("v", lambda v: v.tensor_tensor(
            out=ropeA[:, :], in0=pqs(0, 64), in1=cosT[:, :], op=ALU.mult),
            deps + [t_cos])
        t_r2 = pb.do("v", lambda v: v.tensor_tensor(
            out=ropeB[:, :], in0=pqs(64, 128), in1=sinT[:, :], op=ALU.mult),
            deps + [t_sin])
        t_r3 = pb.do("v", lambda v: v.tensor_tensor(
            out=dstT[0:64, 0:R], in0=ropeA[:, :], in1=ropeB[:, :], op=ALU.subtract),
            [t_r1, t_r2])
        t_r4 = pb.do("v", lambda v: v.tensor_tensor(
            out=ropeA[:, :], in0=pqs(64, 128), in1=cosT[:, :], op=ALU.mult),
            [t_r2, t_r3] + deps)
        t_r5 = pb.do("v", lambda v: v.tensor_tensor(
            out=ropeB[:, :], in0=pqs(0, 64), in1=sinT[:, :], op=ALU.mult),
            [t_r3] + deps)
        t_r6 = pb.do("v", lambda v: v.tensor_tensor(
            out=dstT[64:128, 0:R], in0=ropeA[:, :], in1=ropeB[:, :], op=ALU.add),
            [t_r4, t_r5])
        return [t_r3, t_r6]

    def proj_group(lhsT_fn, bank, deps0):
        def grp(te):
            last = None
            for k in range(NKC):
                for hh in range(NHALF_R):
                    w = min(512, R - 512 * hh)
                    last = te.matmul(
                        out=pQ[:, bank, 512 * hh:512 * hh + w],
                        lhsT=lhsT_fn(k),
                        rhs=ln2T[:, k, :, :].rearrange("p t f -> p (t f)")[:, 512 * hh:512 * hh + w],
                        start=(k == 0), stop=(k == NKC - 1))
            return last
        return pb.do("t", grp, deps0)

    tok_wkv = []
    for k in range(NKC):
        t_sl = pb.dma(g0, slab[:, k % 2, 0:384], wkvT[k, :, :],
                      deps=([tok_wkv[k - 2]] if k >= 2 else tok_wf[-3:]))
        tok_wkv.append(pb.do("v", lambda v, k=k: v.tensor_copy(
            out=wkvb[:, k, :], in_=slab[:, k % 2, 0:384]), [t_sl]))

    tk0 = proj_group(lambda k: wkvb[:, k, 0:128], 0, [tok_wkv[-1], tok_ln2T[-1]])
    tok_kp = emit_rope(lambda a, b: pQ[a:b, 0, 0:R], kprT_own, [tk0])
    tk1 = proj_group(lambda k: wkvb[:, k, 128:256], 1, [tok_wkv[-1], tok_ln2T[-1]])
    t_vc = pb.do("v", lambda v: v.tensor_copy(out=qTbf[:, 0:R], in_=pQ[:, 1, 0:R]),
                 [tk1])
    t_vtr = pb.dmat(vp_own[:, :, :], qTbf[:, 0:R], [t_vc])
    tk2 = proj_group(lambda k: wkvb[:, k, 256:384], 0, [tok_kp[1]])
    t_kvc = pb.do("v", lambda v: v.tensor_copy(out=exbkv[:, 0:R],
                                               in_=pQ[0:32, 0, 0:R]), [tk2])
    t_sg = pb.do("s", lambda s: s.activation(out=exf_own[0:32, 0:R],
                                             in_=pQ[32:64, 0, 0:R], func=ACT.Sigmoid,
                                             bias=0.0, scale=1.0), [tk2])
    t_ls = pb.do("s", lambda s: s.activation(out=exf_own[0:32, 0:R],
                                             in_=exf_own[0:32, 0:R], func=ACT.Ln,
                                             bias=0.0, scale=1.0), [t_sg])
    t_hi = pb.do("v", lambda v: v.tensor_copy(out=exb2[:, 0, 0:R],
                                              in_=exf_own[0:32, 0:R]), [t_ls])
    t_lo = pb.do("v", lambda v: v.scalar_tensor_tensor(
        out=exb2[:, 1, 0:R], in0=exb2[:, 0, 0:R], scalar=-1.0,
        in1=exf_own[0:32, 0:R], op0=ALU.mult, op1=ALU.add), [t_hi])

    t_e1 = pb.dma(g0, ex_in_bf.ap()[0:128, :], kprT_own[:, 0:R], tok_kp)
    t_e2 = pb.dma(g0, ex_in_bf.ap()[128:256, :],
                  vp_own[:, :, :].rearrange("p t f -> p (t f)"), [t_vtr])
    t_e3 = pb.dma(g0, ex_in_bf.ap()[256:320, :]
                  .rearrange("(p c) f -> p c f", c=2), exb2[:, :, 0:R],
                  [t_hi, t_lo])
    t_e4 = pb.dma(g0, ex_in_bf.ap()[320:352, :], exbkv[:, 0:R], [t_kvc])
    t_cc2 = pb.do("g", lambda g: g.collective_compute(
        "AllGather", ALU.bypass, replica_groups=rg, ins=[ex_in_bf.ap().opt()],
        outs=[ex_out_bf.ap().opt()]), [t_e1, t_e2, t_e3, t_e4], sem=cc)
    t_cc1 = t_cc2

    tok_wqb = []
    for k in range(NKC):
        t_sl = pb.dma(g1, slab[:, k % 2, :], wqT[k, :, :],
                      deps=([tok_wqb[k - 2]] if k >= 2 else [tok_wkv[-1]]))
        tok_wqb.append(pb.do("v", lambda v, k=k: v.tensor_copy(
            out=wqb[:, k, :], in_=slab[:, k % 2, :]), [t_sl]))
    tok_qT = [None] * H
    pq_free = [[t_kvc, t_sg], [t_vc]]
    qtbf_free = t_vtr
    for hd in range(H):
        bank = hd % 2
        tk = proj_group(lambda k, hd=hd: wqb[:, k, 128 * hd:128 * (hd + 1)], bank,
                        [tok_wqb[-1]] + (pq_free[bank] if isinstance(pq_free[bank], list)
                                         else [pq_free[bank]]))
        r36 = emit_rope(lambda a, b, bank=bank: pQ[a:b, bank, 0:R], qTbf,
                        [tk] + ([qtbf_free] if qtbf_free else []))
        pq_free[bank] = r36[1]
        tok_qT[hd] = pb.dma(y1, qT_hbm[hd, :, 0:R], qTbf[:, 0:R], r36)
        qtbf_free = tok_qT[hd]
        if dbg:
            qtbf_free = pb.dma(g2, dbg_qt[hd, :, :], qTbf[:, 0:R],
                               r36 + [tok_qT[hd]])

    fenceB0 = [t_cc2, t_cc1, tok_kp[1], t_vtr, t_ls, tok_qT[-1]]
    for pf in pq_free:
        fenceB0 += pf if isinstance(pf, list) else [pf]
    pb.fence(fenceB0)
    stQ.close()
    stB.close()
    stLN.close()
    stA1.close()

    stBC = ExitStack()
    SBC = lambda n, s, dt=F32: stBC.enter_context(nc.sbuf_tensor(n, s, dt))
    masks = SBC("masks", [128, 1024], BF16)
    kprT_all = SBC("kprT_all", [128, NT, 128], BF16)
    vp_all = SBC("vp_all", [128, NT, 128], BF16)

    stIO = ExitStack()
    iota_sb = stIO.enter_context(nc.sbuf_tensor("iota_sb", [128, 1024], F32))
    tok_iota = pb.do("g", lambda g: g.iota(out=iota_sb[:, :], pattern=[[1, 1024]], base=0,
                                           channel_multiplier=0,
                                           allow_small_or_imprecise_dtypes=True))
    t_mask = pb.do("v", lambda v: v.tensor_scalar(
        out=masks[:, :], in0=iota_sb[:, :], scalar1=posc[:, 0:1], scalar2=MBIG,
        op0=ALU.is_gt, op1=ALU.mult), [tok_posc, tok_iota])
    tok_masks = [t_mask] * T

    stB2 = ExitStack()
    SB2 = lambda n, s, dt=F32: stB2.enter_context(nc.sbuf_tensor(n, s, dt))
    glog = SB2("glog", [32, SEQ])
    gkv = SB2("gkv", [32, SEQ])
    E2 = SB2("E2", [32, SEQ])
    khbf = SB2("khbf", [32, SEQ], BF16)

    tok_gather = []
    tok_kvgather = []
    tok_gkv = []
    E2bb = E2[:, :].bitcast(BF16)
    for ci in range(C):
        tok_kvgather.append(pb.dma(
            y0, kprT_all[:, ci::C, :],
            ex_out_bf.ap()[ci, 0:128, :].rearrange("p (t f) -> p t f", f=128),
            [t_cc1]))
        tok_kvgather.append(pb.dma(
            y1, vp_all[:, ci::C, :],
            ex_out_bf.ap()[ci, 128:256, :].rearrange("p (t f) -> p t f", f=128),
            [t_cc1]))
    for ci in range(C):
        tok_gather.append(pb.dma(
            g0, E2bb[:, 0:SEQ].rearrange("p (t f) -> p t f", f=128)[:, ci::C, :],
            ex_out_bf.ap()[ci, 256:320, :].rearrange("(p c) f -> p c f", c=2)[:, 0, :]
            .rearrange("p (t f) -> p t f", f=128), [t_cc2]))
        tok_gather.append(pb.dma(
            g2, E2bb[:, SEQ:2 * SEQ].rearrange("p (t f) -> p t f", f=128)[:, ci::C, :],
            ex_out_bf.ap()[ci, 256:320, :].rearrange("(p c) f -> p c f", c=2)[:, 1, :]
            .rearrange("p (t f) -> p t f", f=128), [t_cc2]))
        tok_gkv.append(pb.dma(
            g3, khbf[:, :].rearrange("p (t f) -> p t f", f=128)[:, ci::C, :],
            ex_out_bf.ap()[ci, 320:352, :].rearrange("p (t f) -> p t f", f=128),
            [t_cc2]))

    t_comb = pb.do("v", lambda v: v.tensor_tensor(
        out=glog[:, :], in0=E2bb[:, 0:SEQ], in1=E2bb[:, SEQ:2 * SEQ],
        op=ALU.add), tok_gather)
    t_gkvf = pb.do("s", lambda s: s.activation(
        out=gkv[:, :], in_=khbf[:, :], func=ACT.Copy, bias=0.0, scale=1.0), tok_gkv)
    t_f1 = pb.do("v", lambda v: v.tensor_tensor_scan(
        out=glog[:, :], data0=glog[:, :], data1=glog[:, :], initial=0.0,
        op0=ALU.add, op1=ALU.bypass), [t_comb])
    t_f2 = pb.do("s", lambda s: s.activation(
        out=E2[:, 1:SEQ], in_=glog[:, 0:SEQ - 1], func=ACT.Exp, bias=0.0, scale=1.0),
        [t_f1])
    t_f3 = pb.do("v", lambda v: v.memset(E2[:, 0:1], 1.0), [t_f1])
    t_f4 = pb.do("v", lambda v: v.tensor_tensor(
        out=E2[:, :], in0=gkv[:, :], in1=E2[:, :], op=ALU.mult),
        [t_f3, t_f2, t_gkvf])
    t_f5 = pb.do("v", lambda v: v.tensor_tensor_scan(
        out=glog[:, :], data0=E2[:, :], data1=E2[:, :], initial=0.0,
        op0=ALU.add, op1=ALU.bypass), [t_f4])
    khvh = glog
    t_dbgkh = None
    if dbg:
        t_dbgkh = pb.dma(g2, dbg_khvh[:, :], khvh[:, :], [t_f5])

    t_khb = pb.do("v", lambda v: v.tensor_copy(out=khbf[:, :], in_=khvh[:, :]), [t_f5])
    t_khd = pb.dma(g0, khT_hbm.ap()[:, :], khbf[16:32, :], [t_khb])
    t_vht = pb.dmat(vh_tm[:, :, :], khbf[0:16, :], [t_khb])
    tok_khbc = {}
    HSPL = 1024 * max(1, T // 2)
    khbc0a = [None]

    def emit_khbc(h_):
        if h_ < H and h_ not in tok_khbc:
            if h_ == 0 and HSPL < SEQ:
                khbc0a[0] = pb.dma(
                    g3, khrep_hbm[0, :, 0:HSPL],
                    khT_hbm.ap()[0, 0:HSPL].partition_broadcast(128), [t_khd])
                tok_khbc[0] = pb.dma(
                    g3, khrep_hbm[0, :, HSPL:SEQ],
                    khT_hbm.ap()[0, HSPL:SEQ].partition_broadcast(128), [t_khd])
            else:
                tok_khbc[h_] = pb.dma(
                    g3, khrep_hbm[h_, :, :],
                    khT_hbm.ap()[h_, :].partition_broadcast(128), [t_khd])
    emit_khbc(0)
    emit_khbc(1)

    pb.fence([t_khd, t_vht, t_khb, tok_masks[-1], t_f5, t_dbgkh, t_comb, t_gkvf]
             + tok_gather + tok_gkv)
    stB2.close()
    stIO.close()

    stC = ExitStack()
    SC = lambda n, s, dt=F32: stC.enter_context(nc.sbuf_tensor(n, s, dt))
    PC = lambda n, s, dt=F32: stC.enter_context(nc.psum_tensor(n, s, dt))
    panel = SC("panel", [128, 2, KRMAXH])
    ptil = SC("ptil", [128, 2, 2, KRMAXH], BF16)
    PT = SC("PT", [128, 2, 2, KRMAXH // 128, 128], BF16)
    qTb = SC("qTb", [128, 2, R], BF16)
    KhrT = SC("KhrT", [128, 2, SEQ], BF16)
    Vh = SC("Vh", [128, 2, NT, 128], BF16)
    mslot = SC("mslot", [128, 2 * T])
    msc = SC("msc", [128, 2, 12])
    osc = SC("osc", [128, 128])
    pS = PC("pS", [128, 2, 1024])
    pO = PC("pO", [128, 2, 2, 512])
    AM_A, AM_B, AM_NA, AM_NB, AM_LA, AM_LB, AM_RA, AM_RB, AM_M, AM_NM, AM_LI, AM_T2 = range(12)

    def msl(i):
        return msc[:, i:i + 1]

    def emit_builds_dma(h, dk, dkr):
        par = h % 2
        emit_khbc(h + 1)
        return pb.dma(g1, KhrT[:, par, :], khrep_hbm[h, :, :],
                      dk + dkr + [tok_khbc[h]])

    def emit_builds_compute(h, t_kh, dvh):
        par = h % 2
        t_k = pb.do("v", lambda v, par=par: v.scalar_tensor_tensor(
            out=KhrT[:, par, :], in0=kprT_all[:, :, :].rearrange("p t f -> p (t f)"),
            scalar=SCALE, in1=KhrT[:, par, :], op0=ALU.mult, op1=ALU.mult),
            [t_kh] + tok_kvgather)
        t_v = pb.do("v", lambda v, h=h, par=par: v.tensor_tensor(
            out=Vh[:, par, :, :], in0=vp_all[:, :, :],
            in1=vh_tm[:, :, h:h + 1].broadcast_to([128, NT, 128]), op=ALU.mult),
            [t_vht] + dvh + tok_kvgather)
        return t_kh, t_k, t_v

    if HSPL < SEQ:
        NTH = HSPL // 128
        emit_khbc(1)
        _b0a = pb.dma(g1, KhrT[:, 0, 0:HSPL], khrep_hbm[0, :, 0:HSPL],
                      [khbc0a[0]])
        _b0 = pb.dma(g1, KhrT[:, 0, HSPL:SEQ], khrep_hbm[0, :, HSPL:SEQ],
                     [tok_khbc[0]])
        t_k_a = pb.do("v", lambda v: v.scalar_tensor_tensor(
            out=KhrT[:, 0, 0:HSPL],
            in0=kprT_all[:, 0:NTH, :].rearrange("p t f -> p (t f)"),
            scalar=SCALE, in1=KhrT[:, 0, 0:HSPL], op0=ALU.mult, op1=ALU.mult),
            [_b0a] + tok_kvgather)
        t_k_b = pb.do("v", lambda v: v.scalar_tensor_tensor(
            out=KhrT[:, 0, HSPL:SEQ],
            in0=kprT_all[:, NTH:NT, :].rearrange("p t f -> p (t f)"),
            scalar=SCALE, in1=KhrT[:, 0, HSPL:SEQ], op0=ALU.mult, op1=ALU.mult),
            [_b0] + tok_kvgather)
        t_v_a = pb.do("v", lambda v: v.tensor_tensor(
            out=Vh[:, 0, 0:NTH, :], in0=vp_all[:, 0:NTH, :],
            in1=vh_tm[:, 0:NTH, 0:1].broadcast_to([128, NTH, 128]), op=ALU.mult),
            [t_vht] + tok_kvgather)
        t_v_b = pb.do("v", lambda v: v.tensor_tensor(
            out=Vh[:, 0, NTH:NT, :], in0=vp_all[:, NTH:NT, :],
            in1=vh_tm[:, NTH:NT, 0:1].broadcast_to([128, NT - NTH, 128]),
            op=ALU.mult), [t_vht] + tok_kvgather)
        builds = {0: (_b0, {"a": t_k_a, "b": t_k_b}, {"a": t_v_a, "b": t_v_b})}
    else:
        _b0 = emit_builds_dma(0, [t_khd], [t_cc1])
        builds = {0: emit_builds_compute(0, _b0, [t_cc1])}

    def kdep(h, ck):
        b = builds[h][1]
        if isinstance(b, dict):
            return b["a"] if 1024 * (ck + 1) <= HSPL else b["b"]
        return b

    def vdep(h, c0, nch):
        b = builds[h][2]
        if isinstance(b, dict):
            return b["a"] if 1024 * (c0 + nch) <= HSPL else b["b"]
        return b
    tok_q_load = {0: pb.dma(y0, qTb[:, 0, 0:R], qT_hbm[0, :, 0:R], [tok_qT[0]])}

    last_qk_of_head = {}
    tok_exp = {0: None, 1: None}
    tok_pvdone = {}
    tok_odma = {}
    tok_ptil_free = {(a, b): None for a in range(2) for b in range(2)}
    tok_PT_free = {(a, b): None for a in range(2) for b in range(2)}
    vh_last_use = {0: None, 1: None}
    evac_by_bank = {0: None, 1: None}
    o_dma_toks = []
    pending_tail = [None]
    pending_tail2 = [None]
    osc_last = [None]

    def msl2(ip, i):
        return msc[:, ip, i:i + 1]

    def emit_front(h, j, mid_hook=None):
        par = h % 2
        ip = (h * T + j) % 2
        gp = ip
        n = j + 1
        ha = (n + 1) // 2
        halves = [(0, ha)] if n == ha else [(0, ha), (ha, n)]
        half_info = []
        last_mm = None
        for hi, (c0, c1) in enumerate(halves):
            nch = c1 - c0
            ev_toks = []
            for ci, ck in enumerate(range(c0, c1)):
                bank = ck % 2

                def qkgrp(te, j=j, ck=ck, bank=bank, par=par):
                    last = None
                    for s2 in range(2):
                        last = te.matmul(
                            out=pS[:, bank, 512 * s2:512 * (s2 + 1)],
                            lhsT=qTb[:, par, 128 * j:128 * (j + 1)],
                            rhs=KhrT[:, par, 1024 * ck + 512 * s2:1024 * ck + 512 * (s2 + 1)],
                            start=True, stop=(ck != j))
                        if ck == j:
                            last = te.matmul(
                                out=pS[:, bank, 512 * s2:512 * (s2 + 1)],
                                lhsT=identb[:, :],
                                rhs=masks[:, 512 * s2:512 * (s2 + 1)],
                                start=False, stop=True)
                    return last
                mmdeps = [kdep(h, ck), tok_q_load[h], evac_by_bank[bank]]
                if ck == j:
                    mmdeps += [tok_identb, tok_masks[j]]
                last_mm = pb.do("t", qkgrp, mmdeps)
                t_ev = pb.do("v", lambda v, hi=hi, ci=ci, bank=bank: v.tensor_scalar(
                    out=panel[:, hi, 1024 * ci:1024 * (ci + 1)], in0=pS[:, bank, :],
                    scalar1=1.0, scalar2=NEG, op0=ALU.mult, op1=ALU.max,
                    accum_out=mslot[:, hi * T + ci:hi * T + ci + 1]),
                    [last_mm, tok_exp[hi]])
                ev_toks.append(t_ev)
                evac_by_bank[bank] = t_ev
            nfin = AM_NA if hi == 0 else AM_NB
            lfin = AM_LA if hi == 0 else AM_LB
            t_nm = pb.do("v", lambda v, hi=hi, nch=nch, nfin=nfin, ip=ip:
                         v.tensor_reduce(
                out=msl2(ip, nfin), in_=mslot[:, hi * T:hi * T + nch], axis=AXX,
                op=ALU.max, negate=True), ev_toks)
            kr = 1024 * nch
            t_exp = pb.do("s", lambda s, hi=hi, kr=kr, nfin=nfin, lfin=lfin, ip=ip:
                          s.activation(
                out=ptil[:, ip, hi, 0:kr], in_=panel[:, hi, 0:kr], func=ACT.Exp,
                bias=msl2(ip, nfin), scale=1.0, accum_out=msl2(ip, lfin)),
                [t_nm, tok_ptil_free[(ip, hi)]] + ev_toks)
            tok_exp[hi] = t_exp
            t_tr = pb.dmat(PT[:, ip, hi, 0:kr // 128, :], ptil[:, ip, hi, 0:kr],
                           [t_exp, tok_PT_free[(ip, hi)]])
            tok_ptil_free[(ip, hi)] = t_tr
            half_info.append((t_tr, t_exp, nch, hi, c0))
            if hi == 0 and mid_hook is not None:
                mid_hook()
        last_qk_of_head[h] = last_mm
        return (h, j, par, ip, gp, halves, half_info)

    def emit_tail_stats(state):
        (h, j, par, ip, gp, halves, half_info) = state
        if len(halves) == 1:
            return None
        t_nm2 = pb.do("v", lambda v, ip=ip: v.tensor_tensor(
            out=msl2(ip, AM_NM), in0=msl2(ip, AM_NA), in1=msl2(ip, AM_NB),
            op=ALU.min), [half_info[0][1], half_info[1][1]])
        t_ra = pb.do("s", lambda s, ip=ip: s.activation(
            out=msl2(ip, AM_RA), in_=msl2(ip, AM_NA), func=ACT.Exp,
            bias=msl2(ip, AM_NM), scale=-1.0), [t_nm2, half_info[0][1]])
        t_rb = pb.do("s", lambda s, ip=ip: s.activation(
            out=msl2(ip, AM_RB), in_=msl2(ip, AM_NB), func=ACT.Exp,
            bias=msl2(ip, AM_NM), scale=-1.0), [t_ra, t_nm2, half_info[1][1]])
        t_l1 = pb.do("s", lambda s, ip=ip: s.activation(
            out=msl2(ip, AM_T2), in_=msl2(ip, AM_LA), func=ACT.Copy, bias=0.0,
            scale=msl2(ip, AM_RA)), [t_ra, half_info[0][1]])
        return (t_ra, t_rb, t_l1)

    def emit_tail(state, stats):
        (h, j, par, ip, gp, halves, half_info) = state
        pv_toks = []
        for (t_tr, t_exp2, nch, hi, c0) in half_info:
            def pvgrp(te, hi=hi, nch=nch, c0=c0, gp=gp, par=par, ip=ip):
                last = None
                for kt in range(nch * 8):
                    last = te.matmul(
                        out=pO[:, gp, hi, 0:128], lhsT=PT[:, ip, hi, kt, :],
                        rhs=Vh[:, par, c0 * 8 + kt, :],
                        start=(kt == 0), stop=(kt == nch * 8 - 1))
                return last
            t_pv = pb.do("t", pvgrp, [t_tr, vdep(h, c0, nch), tok_pvdone.get(gp)])
            pv_toks.append(t_pv)
            tok_PT_free[(ip, hi)] = t_pv
            vh_last_use[par] = t_pv
        lslot = AM_LI if ((h * T + j) >> 1) & 1 == 0 else AM_M
        if len(halves) == 1:
            t_li = pb.do("v", lambda v, ip=ip, lslot=lslot: v.reciprocal(
                out=msl2(ip, lslot), in_=msl2(ip, AM_LA)), [half_info[0][1]])
            stats2 = (None, None, t_li, lslot)
        else:
            (t_ra, t_rb, t_l1) = stats
            t_l2 = pb.do("v", lambda v, ip=ip, lslot=lslot: v.scalar_tensor_tensor(
                out=msl2(ip, lslot), in0=msl2(ip, AM_LB), scalar=msl2(ip, AM_RB),
                in1=msl2(ip, AM_T2), op0=ALU.mult, op1=ALU.add),
                [t_rb, t_l1, half_info[1][1]])
            t_li = pb.do("v", lambda v, ip=ip, lslot=lslot: v.reciprocal(
                out=msl2(ip, lslot), in_=msl2(ip, lslot)), [t_l2])
            stats2 = (t_ra, t_rb, t_li, lslot)
        return (h, j, par, ip, gp, halves, half_info, stats2, pv_toks)

    def emit_tail2(state2):
        (h, j, par, ip, gp, halves, half_info, stats, pv_toks) = state2
        (t_ra, t_rb, t_li, lslot) = stats
        if len(halves) == 1:
            t_oe = pb.do("v", lambda v, gp=gp, ip=ip, lslot=lslot: v.tensor_scalar(
                out=o_sb[:, gp, :], in0=pO[:, gp, 0, 0:128], scalar1=msl2(ip, lslot),
                scalar2=0.0, op0=ALU.mult, op1=ALU.add),
                [t_li, tok_odma.get(gp)] + pv_toks)
        else:
            t_o1 = pb.do("v", lambda v, gp=gp, ip=ip: v.tensor_scalar(
                out=osc[:, :], in0=pO[:, gp, 0, 0:128], scalar1=msl2(ip, AM_RA),
                scalar2=0.0, op0=ALU.mult, op1=ALU.add),
                [t_ra, osc_last[0]] + pv_toks[:1])
            t_o2 = pb.do("v", lambda v, gp=gp, ip=ip: v.scalar_tensor_tensor(
                out=osc[:, :], in0=pO[:, gp, 1, 0:128], scalar=msl2(ip, AM_RB),
                in1=osc[:, :], op0=ALU.mult, op1=ALU.add),
                [t_rb, t_o1] + pv_toks[1:])
            t_oe = pb.do("v", lambda v, gp=gp, ip=ip, lslot=lslot: v.tensor_scalar(
                out=o_sb[:, gp, :], in0=osc[:, :], scalar1=msl2(ip, lslot),
                scalar2=0.0, op0=ALU.mult, op1=ALU.add),
                [t_li, t_o2, tok_odma.get(gp)])
            osc_last[0] = t_oe
        tok_pvdone[gp] = t_oe
        tok_odma[gp] = pb.dma(y1, o_hbm[j, :, 128 * h:128 * (h + 1)],
                              o_sb[:, gp, :], [t_oe])
        o_dma_toks.append(tok_odma[gp])

    for h in range(H):
        if h + 1 < H:
            tok_q_load[h + 1] = pb.dma(y0, qTb[:, (h + 1) % 2, 0:R],
                                       qT_hbm[h + 1, :, 0:R],
                                       [tok_qT[h + 1], last_qk_of_head.get(h - 1)])
        for j in range(T):
            box = {}

            def hook(box=box):
                if pending_tail[0] is not None:
                    box["stats"] = emit_tail_stats(pending_tail[0][0])
            st = emit_front(h, j, mid_hook=hook)
            st2 = None
            if pending_tail[0] is not None:
                st2 = emit_tail(pending_tail[0][0], box.get("stats"))
            if pending_tail2[0] is not None:
                emit_tail2(pending_tail2[0])
            pending_tail2[0] = st2
            pending_tail[0] = (st,)
            if j == min(1, T - 1) and h + 1 < H:
                builds[h + 1] = (emit_builds_dma(
                    h + 1, [last_qk_of_head.get(h - 1)],
                    [last_qk_of_head.get(h - 1)]),)
            if j == min(5, T - 1) and h + 1 < H:
                builds[h + 1] = emit_builds_compute(
                    h + 1, builds[h + 1][0], [vh_last_use.get((h + 1) % 2)])
    _fs = emit_tail_stats(pending_tail[0][0])
    _st2 = emit_tail(pending_tail[0][0], _fs)
    if pending_tail2[0] is not None:
        emit_tail2(pending_tail2[0])
    emit_tail2(_st2)
    if dbg:
        for t in range(T):
            pb.dma(g2, dbg_o[t, :, :], o_hbm[t, :, :], o_dma_toks)

    pb.fence(o_dma_toks + [tok_pvdone.get(0), tok_pvdone.get(1)])

    stC.close()
    stBC.close()

    stD = ExitStack()
    SD = lambda n, s, dt=F32: stD.enter_context(nc.sbuf_tensor(n, s, dt))
    PD = lambda n, s, dt=F32: stD.enter_context(nc.psum_tensor(n, s, dt))
    wob = SD("wob", [128, NKC, D], BF16)
    oT = SD("oT", [128, 2, NKC, 128], BF16)
    x1t = SD("x1t", [128, 2, D])
    x2t = SD("x2t", [128, 2, D])
    xn3 = SD("xn3", [128, 2, D])
    d_v = SD("d_v", [128, 2048]); d_b1 = SD("d_b1", [128, 2048])
    xq3 = SD("xq3", [128, 2, D], BF16)
    xq3T = SD("xq3T", [128, 2, NKC, 128], BF16)
    fac2 = SD("fac2", [128, T])
    slabD = SD("slabD", [128, 3, D])
    pD = PD("pD", [128, 2, 512])

    tok_wob = []
    for k in range(NKC):
        t_sl = pb.dma([g0, g1, g2][k % 3], slabD[:, k % 3, :], woT[k, :, :],
                      deps=[tok_wob[k - 3]] if k >= 3 else [])
        tok_wob.append(pb.do("v", lambda v, k=k: v.tensor_copy(
            out=wob[:, k, :], in_=slabD[:, k % 3, :]), [t_sl]))

    tok_x2 = [None] * T
    d2_mm_last = [None] * 4
    SPT = NKC * 128 // R
    bankD_free = [None, None]
    d2_free = {0: None, 1: None}
    d3_ln_toks = [None] * T
    d3_xn_free = {0: None, 1: None}
    d3_xq_free = {0: None, 1: None}
    xq3T_free = {0: None, 1: None}
    x2_readers = {0: None, 1: None}
    tok_xq_sp = [None] * T
    tok_fac2 = [None] * T

    def d2_frontf(t):
        tb = t % 2
        t_ot = pb.dmat(oT[:, tb, :, :], o_hbm[t, :, :],
                       o_dma_toks[-3:] + [d2_free[tb]])
        t_x1l = pb.dma(y0, x1t[:, tb, :], x1_hbm[t, :, :], [tok_x1[t], d2_free[tb]])
        return (t, tb, t_ot, t_x1l)

    def d2_tailf(state):
        t, tb, t_ot, t_x1l = state
        epi = []
        for og in range(4):
            bank = og % 2

            def mmo(te, og=og, bank=bank, tb=tb):
                last = None
                for k in range(NKC):
                    last = te.matmul(out=pD[:, bank, :], lhsT=oT[:, tb, k, :],
                                     rhs=wob[:, k, 512 * og:512 * (og + 1)],
                                     start=(k == 0), stop=(k == NKC - 1))
                return last
            tk = pb.do("t", mmo, [t_ot, bankD_free[bank]] + tok_wob[-3:])
            d2_mm_last[og] = tk
            t_epi = pb.do("v", lambda v, og=og, bank=bank, tb=tb: v.scalar_tensor_tensor(
                out=x2t[:, tb, 512 * og:512 * (og + 1)], in0=pD[:, bank, :], scalar=1.0,
                in1=x1t[:, tb, 512 * og:512 * (og + 1)], op0=ALU.mult, op1=ALU.add),
                [tk, t_x1l, tok_x2[t - 2] if t >= 2 else None, x2_readers[tb]])
            epi.append(t_epi)
            bankD_free[bank] = t_epi
        tok_x2[t] = pb.dma(y1, x2_hbm[t, :, :], x2t[:, tb, :], epi)
        if dbg:
            tok_x2[t] = pb.dma(g2, dbg_x2[t, :, :], x2t[:, tb, :], epi + [tok_x2[t]])
        d2_free[tb] = epi[-1]
        return epi

    def d3_prep(t, epi):
        tb = t % 2
        t_lno = emit_ln(x2t[:, tb, :], xn3[:, tb, :], [epi[-1]],
                        [d3_xn_free[tb], d3_ln_toks[t - 1] if t else None])
        d3_ln_toks[t] = t_lno
        x2_readers[tb] = t_lno
        t_q, t_b = emit_quant(xn3[:, tb, :], xq3[:, tb, :], d_v[:, :], d_b1[:, :],
                              [t_lno], [d3_xq_free[tb]])
        d3_xn_free[tb] = t_q
        t_f = pb.do("v", lambda v, t=t: v.tensor_tensor(
            out=fac2[:, t:t + 1], in0=slot(SL_BETA), in1=alpha_col[:, :], op=ALU.mult),
            [t_b])
        t_f2 = pb.do("v", lambda v, t=t: v.tensor_scalar(
            out=fac2[:, t:t + 1], in0=fac2[:, t:t + 1], scalar1=0.5, scalar2=0.0,
            op0=ALU.mult, op1=ALU.add), [t_f])
        tok_fac2[t] = t_f2
        t_tr = pb.dmat(xq3T[:, tb, :, :], xq3[:, tb, :], [t_q, xq3T_free[tb]])
        d3_xq_free[tb] = t_tr
        tok_xq_sp[t] = pb.dma(
            g2, qT_hbm.ap()[SPT * t:SPT * (t + 1), :, :]
            .rearrange("a p (c f) -> p a c f", f=128),
            xq3T[:, tb, :, :].rearrange("p (a c) f -> p a c f", a=SPT), [t_tr])
        xq3T_free[tb] = tok_xq_sp[t]

    pend = None
    for t in range(T):
        st = d2_frontf(t)
        if pend is not None:
            ep = d2_tailf(pend)
            d3_prep(pend[0], ep)
        pend = st
    ep = d2_tailf(pend)
    d3_prep(pend[0], ep)
    t_d2last = bankD_free[1] if bankD_free[1] is not None else bankD_free[0]

    tok_out = [None] * T
    d3_free = {0: None, 1: None}
    d3_x1_free = dict(d2_free)
    d3_xqT_free = dict(xq3T_free)
    x2w_guard = {0: [tok_x2[T - 2] if T >= 2 else None,
                     d3_ln_toks[T - 2] if T >= 2 else None],
                 1: [tok_x2[T - 1], d3_ln_toks[T - 1]]}

    def d3_frontf(t):
        tb = t % 2
        t_xr = pb.dma(
            g3, xq3T[:, tb, :, :].rearrange("p (a c) f -> p a c f", a=SPT),
            qT_hbm.ap()[SPT * t:SPT * (t + 1), :, :]
            .rearrange("a p (c f) -> p a c f", f=128),
            [tok_xq_sp[t], d3_xqT_free[tb]])
        t_x2r = pb.dma(y0, x1t[:, tb, :], x2_hbm[t, :, :],
                       [tok_x2[t], d3_x1_free[tb]])
        return (t, tb, t_xr, t_x2r)

    def d3_tailf(state):
        t, tb, t_xr, t_x2r = state
        epi = []
        for og in range(4):
            bank = og % 2

            def mmf(te, og=og, bank=bank, tb=tb):
                last = None
                for k in range(NKC):
                    last = te.matmul(out=pD[:, bank, :], lhsT=xq3T[:, tb, k, :],
                                     rhs=wob[:, k, 512 * og:512 * (og + 1)],
                                     start=(k == 0), stop=(k == NKC - 1))
                return last
            tk = pb.do("t", mmf, [t_xr, bankD_free[bank]] + wqf2_og[og])
            t_epi = pb.do("v", lambda v, og=og, bank=bank, t=t, tb=tb:
                          v.scalar_tensor_tensor(
                out=x2t[:, tb, 512 * og:512 * (og + 1)], in0=pD[:, bank, :],
                scalar=fac2[:, t:t + 1], in1=x1t[:, tb, 512 * og:512 * (og + 1)],
                op0=ALU.mult, op1=ALU.add),
                [tk, t_x2r, tok_fac2[t], d3_free[tb]] + x2w_guard[tb])
            epi.append(t_epi)
            bankD_free[bank] = t_epi
        d3_xqT_free[tb] = epi[-1]
        tok_out[t] = pb.dma(y1, out[t, :, :], x2t[:, tb, :], epi)
        d3_free[tb] = tok_out[t]
        d3_x1_free[tb] = epi[-1]
        return tok_out[t]

    pend = d3_frontf(0)

    tok_wqf2 = []
    wqf2_og = [None] * 4
    for og in range(4):
        for k in range(NKC):
            tok_wqf2.append(pb.dma(
                g1, wob[:, k, 512 * og:512 * (og + 1)], wfbf_hbm[og, k, :, :],
                deps=(tok_wob[-3:] + [d2_mm_last[og]]) if k == 0 else
                [d2_mm_last[og]] if k <= 2 else []))
        wqf2_og[og] = tok_wqf2[-3:]

    for t in range(1, T):
        st = d3_frontf(t)
        d3_tailf(pend)
        pend = st
    d3_tailf(pend)

    pb.do("y", lambda y: None, tok_out)

    pb.emit()
    stD.close()
    es.close()
    return nc


def shard_inputs(L0_attn_x, L0_attn_Wq, L0_attn_Wkv, L0_attn_Wkvh, L0_attn_Wg,
                 L0_attn_Wout, L0_ffn_weight, T=8):
    x = np.asarray(L0_attn_x, np.float32).reshape(C * T, 128, D)
    wq = np.ascontiguousarray(np.asarray(L0_attn_Wq, np.float32).T).reshape(NKC, 128, D)
    wo = np.ascontiguousarray(np.asarray(L0_attn_Wout, np.float32).T).reshape(NKC, 128, D)
    wf = np.ascontiguousarray(np.asarray(L0_ffn_weight, np.float32).T).reshape(NKC, 128, D)
    wkvh = np.asarray(L0_attn_Wkvh, np.float32)
    wg = np.asarray(L0_attn_Wg, np.float32)
    perm = np.concatenate([np.arange(16, 32), np.arange(0, 16)])
    wkv_cat = np.concatenate([np.asarray(L0_attn_Wkv, np.float32),
                              wkvh[perm], wg[perm],
                              np.zeros((64, D), np.float32)], axis=0)
    wkvt = np.ascontiguousarray(wkv_cat.T).reshape(NKC, 128, 384)
    import ml_dtypes
    identb = np.eye(128, dtype=ml_dtypes.bfloat16)
    in_maps = []
    for c in range(C):
        tiles = x[c::C]
        gidx = (np.arange(T) * C + c)
        pos = (gidx[:, None] * 128 + np.arange(128)[None, :]).astype(np.float32)
        in_maps.append({
            "xin": np.ascontiguousarray(tiles),
            "wqT": wq, "woT": wo, "wkvT": wkvt, "wfT": wf,
            "pos_row": np.ascontiguousarray(pos.reshape(1, T * 128)),
            "pos_col": np.ascontiguousarray(pos.T),
            "identb_in": identb,
        })
    return in_maps


def kernel(L0_attn_x, L0_attn_Wq, L0_attn_Wkv, L0_attn_Wkvh, L0_attn_Wg,
           L0_attn_Wout, L0_ffn_weight):
    T = 8
    in_maps = shard_inputs(L0_attn_x, L0_attn_Wq, L0_attn_Wkv, L0_attn_Wkvh,
                           L0_attn_Wg, L0_attn_Wout, L0_ffn_weight, T=T)
    nc = build_program(T=T)
    res = run_bass_kernel_spmd(nc, in_maps, core_ids=list(range(C)))
    full = np.empty((C * T, 128, D), np.float32)
    for c in range(C):
        full[c::C] = np.asarray(res.results[c]["out"], np.float32)
    return full.reshape(1, C * T * 128, D)
